# version 01
# baseline (speedup 1.0000x reference)
import numpy as np
import ml_dtypes
import concourse.bass as bass
import concourse.bacc as bacc
import concourse.mybir as mybir
from concourse.bass_utils import run_bass_kernel_spmd

F32 = mybir.dt.float32
BF16 = mybir.dt.bfloat16
I16 = mybir.dt.int16
I32 = mybir.dt.int32
BF = ml_dtypes.bfloat16
AL = mybir.AluOpType
AF = mybir.ActivationFunctionType
AX = mybir.AxisListType.X

NCORES = 8
T = 4096
D = 1024
E = 16
H = 16
NT = T // 128
NTILE = 80
NPOS = NTILE * 128
SEQ = 2048
NSEQT = SEQ // 128

STAGE = 3
PHASES = 99


def _bfb(x):
    return np.asarray(x, np.float32).astype(BF).view(np.uint16)


class Arena:
    def __init__(self, nc, name, nbytes):
        self.h = nc.alloc_sbuf_tensor(name, [128, nbytes], mybir.dt.uint8)
        self.base = self.h.ap()
        self.off = 0
        self.nbytes = nbytes

    def take(self, shape, dtype, name=""):
        esz = mybir.dt.size(dtype)
        free = int(np.prod(shape[1:]))
        nb = free * esz
        self.off = (self.off + 63) // 64 * 64
        assert self.off + nb <= self.nbytes, f"arena overflow {name}: {self.off}+{nb}"
        v = self.base[0:shape[0], self.off:self.off + nb].bitcast(dtype)
        self.off += nb
        if len(shape) == 2:
            return v
        spec = " ".join(f"d{i}" for i in range(1, len(shape)))
        kw = {f"d{i}": shape[i] for i in range(1, len(shape))}
        return v.rearrange(f"p ({spec}) -> p {spec}", **kw)


def build(stage, dbg=False):
    nc = bacc.Bacc("TRN2", target_bir_lowering=False, debug=False,
                   detect_race_conditions=False)
    IK = "ExternalOutput" if dbg else "Internal"

    x_in = nc.dram_tensor("x", [T, D], F32, kind="ExternalInput").ap()
    router_d = nc.dram_tensor("router_w", [D, E], F32, kind="ExternalInput").ap()
    gud_d = [nc.dram_tensor(nm, [E, D, D], F32, kind="ExternalInput").ap()
             for nm in ("gate_w", "up_w", "down_w")]
    wv2_d = nc.dram_tensor("wv2", [D, D], F32, kind="ExternalInput").ap()
    wo2_d = nc.dram_tensor("wo2", [D, D], F32, kind="ExternalInput").ap()
    k123_d = nc.dram_tensor("k123", [D, 48], F32, kind="ExternalInput").ap()
    abc_d = nc.dram_tensor("abc", [5, D], F32, kind="ExternalInput").ap()
    out_d = nc.dram_tensor("out", [T, D], F32, kind="ExternalOutput").ap()

    xres = nc.dram_tensor("xres", [T, D], F32, kind="Internal").ap()
    xln = nc.dram_tensor("xln_s", [T, D], BF16, kind=IK).ap()
    xe_s = nc.dram_tensor("xe_s", [NPOS, D], BF16, kind=IK).ap()
    outs_s = nc.dram_tensor("outs_s", [NPOS, D], BF16, kind=IK).ap()
    qdram = nc.dram_tensor("qdram", [2 * T], I16, kind="Internal").ap()
    wgbf = nc.dram_tensor("wgbf", [3, E, D, D], BF16, kind="Internal").ap()
    gbdram = nc.dram_tensor("gbdram", [2 * T], I16, kind=IK).ap()

    ident_c = nc.inline_tensor(_bfb(np.eye(128)), name="c_ident").ap().bitcast(BF16)
    id16f_c = nc.inline_tensor(np.eye(16, dtype=np.float32), name="c_id16f").ap()
    tri_c = nc.inline_tensor(_bfb(np.triu(np.ones((128, 128)))), name="c_tri").ap().bitcast(BF16)
    trix_c = nc.inline_tensor(_bfb(np.triu(np.ones((128, 128)), 1)), name="c_trix").ap().bitcast(BF16)
    onesc_c = nc.inline_tensor(_bfb(np.ones((128, 1))), name="c_onesc").ap().bitcast(BF16)
    onesrf_c = nc.inline_tensor(np.ones((1, 128), np.float32), name="c_onesrf").ap()
    iota16_c = nc.inline_tensor(
        np.broadcast_to(np.arange(16, dtype=np.float32), (128, 16)).copy(), name="c_iota16").ap()
    iotaj_c = nc.inline_tensor(np.arange(NTILE, dtype=np.float32).reshape(1, NTILE),
                               name="c_iotaj").ap()
    wi = np.zeros((128, NPOS // 16), np.int16)
    for j in range(NPOS):
        wi[j % 16, j // 16] = j
    for g in range(1, 8):
        wi[g * 16:(g + 1) * 16] = wi[:16]
    widx_c = nc.inline_tensor(wi, name="c_widx").ap()
    maskE_c = nc.inline_tensor(
        (np.arange(NTILE) % 2 == 0).astype(np.float32).reshape(1, NTILE), name="c_maskE").ap()
    maskO_c = nc.inline_tensor(
        (np.arange(NTILE) % 2 == 1).astype(np.float32).reshape(1, NTILE), name="c_maskO").ap()
    ncol_c = nc.inline_tensor(
        (np.arange(SEQ, dtype=np.float32) / (SEQ - 1)).reshape(NSEQT, 128).T.copy(),
        name="c_ncol").ap()

    outer = Arena(nc, "outer", 44 * 1024)
    ident = outer.take([128, 128], BF16, "ident")
    id16f = outer.take([16, 16], F32, "id16f")
    tri = outer.take([128, 128], BF16, "tri")
    trix = outer.take([128, 128], BF16, "trix")
    onesc = outer.take([128, 1], BF16, "onesc")
    onesr = outer.take([1, 128], F32, "onesr")
    iota16 = outer.take([128, 16], F32, "iota16")
    iotaj = outer.take([1, NTILE], F32, "iotaj")
    maskE = outer.take([1, NTILE], F32, "maskE")
    maskO = outer.take([1, NTILE], F32, "maskO")
    wbase = outer.take([1, 8], F32, "wbase")
    jmark = outer.take([128, 8], BF16, "jmark")
    widx = outer.take([128, NPOS // 16], I16, "widx")
    ncol = outer.take([128, NSEQT], F32, "ncol")
    routsb = outer.take([128, 8, E], BF16, "routsb")
    k123sb = outer.take([128, 8, 48], BF16, "k123sb")
    p1a = outer.take([128, NSEQT, H], F32, "p1a")
    p2ea = outer.take([128, NSEQT, H], F32, "p2ea")
    zeros2k = outer.take([128, 2048], BF16, "zeros2k")
    junk = outer.take([128, 8], F32, "junk")
    w1a = outer.take([128, NT], F32, "w1a")
    w2a = outer.take([128, NT], F32, "w2a")
    i1a = outer.take([128, NT], F32, "i1a")
    i2a = outer.take([128, NT], F32, "i2a")
    eqall = outer.take([128, 2, E, NT], BF16, "eqall")
    qf = outer.take([128, 2, NT], F32, "qf")
    q16 = outer.take([128, 2, NT], I16, "q16")
    scat_idx = outer.take([128, 512], I16, "scat_idx")
    gb_idx = outer.take([128, 512], I16, "gb_idx")
    rowbuf = outer.take([1, 2048], F32, "rowbuf")
    rowbuf2 = outer.take([1, 2048], F32, "rowbuf2")
    tiletab = outer.take([1, 640], I32, "tiletab")
    carry = outer.take([1, 2176], F32, "carry")
    ln_m = outer.take([128, 6], F32, "ln_m")

    main = Arena(nc, "main", 158 * 1024)
    wbuf = main.take([128, 2, 3, 8, D], BF16, "wbuf")
    xeT = main.take([128, 2, 8, 1024], BF16, "xeT")
    silu_sb = main.take([128, D], F32, "silu")
    xt2 = main.take([128, 2, D], F32, "xt2")
    lT_sb = main.take([16, 1024], F32, "lT")
    logits = main.take([128, NT, E], F32, "logits")
    r8_mark = main.off
    lnout = main.take([128, 2, D], BF16, "lnout")
    sq_sb = main.take([128, D], F32, "sq")
    hid_sb = main.base[0:128, r8_mark:r8_mark + 2048].bitcast(BF16)
    hidT_sb = main.base[0:128, r8_mark + 2048:r8_mark + 4096].bitcast(BF16).rearrange(
        "p (a b) -> p a b", a=8)
    ob = main.base[0:128, r8_mark + 4096:r8_mark + 8192].bitcast(BF16).rearrange(
        "p (a b) -> p a b", a=2)
    outt = main.base[0:128, r8_mark:r8_mark + 8192].bitcast(F32).rearrange(
        "p (a b) -> p a b", a=2)
    ffn_end = main.off
    main.off = 0
    at_wv2 = main.take([128, 8, D], BF16, "at_wv2")
    at_wo2 = main.take([128, 8, D], BF16, "at_wo2")
    at_xlnT = main.take([128, 2, 8, 1024], BF16, "at_xlnT")
    at_abcsb = main.take([1, 5 * D], F32, "at_abcsb")
    at_abc = main.take([128, 5, D], F32, "at_abc")
    at_u = main.take([128, D], BF16, "at_u")
    at_ud = main.take([128, 16], BF16, "at_ud")
    at_tvd = main.take([128, D], F32, "at_tvd")
    at_eE = main.take([128, 48], F32, "at_eE")
    at_cde = main.take([128, 48], F32, "at_cde")
    at_cde2 = main.take([128, 48], F32, "at_cde2")
    at_peee = main.take([128, 16], F32, "at_peee")
    at_den = main.take([128, 16], F32, "at_den")
    at_den2 = main.take([128, 16], F32, "at_den2")
    at_rden = main.take([128, 16], F32, "at_rden")
    at_num = main.take([128, D], F32, "at_num")
    at_num2 = main.take([128, D], F32, "at_num2")
    at_outv = main.take([128, D], BF16, "at_outv")
    at_outvT = main.take([128, 8, 128], BF16, "at_outvT")
    at_xt = main.take([128, 2, D], F32, "at_xt")
    at_o = main.take([128, 2, D], F32, "at_o")
    at_lnout = main.take([128, 2, D], BF16, "at_lnout")
    at_sq = main.take([128, D], F32, "at_sq")
    at_sarg = main.take([128, D], F32, "at_sarg")
    at_sinv = main.take([128, D], F32, "at_sinv")
    at_p2r = main.take([128, NSEQT, H], F32, "at_p2r")

    pA = nc.alloc_psum_tensor("pA", [128, 1024], F32)
    pB = nc.alloc_psum_tensor("pB", [128, 1024], F32)
    pC = nc.alloc_psum_tensor("pC", [128, 1024], F32)
    pD = nc.alloc_psum_tensor("pD", [128, 512], F32)
    pbf = nc.alloc_psum_tensor("pbf", [128, 1024], BF16)

    sems = {}
    base = {}

    def S(name):
        if name not in sems:
            sems[name] = nc.alloc_semaphore(name)
        return sems[name]

    def B(name):
        return base.get(name, 0)

    def adv(totals):
        for k, vv in totals.items():
            base[k] = base.get(k, 0) + vv

    FFN_TOT = dict(xld0=256, xld1=256, sq=32, lnst=32, rst=32, ln=32,
                   lnw0=256, lnw1=256, zero=640, xg0=32, xg1=32, rlg=4, rev=4,
                   rtr=4, lg=4, tmx=32, texp=32, tm2=32, texp2=32, top=32, eq=1,
                   cs=1, cb=1, rk=1, qd=1, tt=1, tw=384, tr0=48, tr=224,
                   dl0=32, dl1=32, sc0=64, sc1=64, wgu0=1280, wgu1=1280, wd0=640, wd1=640, gt0=80,
                   gt1=80, xeu=80, gd=80, si=80, ve=240, tr2=80, tl=80,
                   obw0=640, obw1=640, gb0=64, gb1=64, cx0=256, cx1=256,
                   cmb=32, ow0=256, ow1=256)
    ATTN_TOT = dict(wl=32, wlh=16, bcm=5, bce=5, varg=32, asin=32, vp2=16, p2e=1, car=2,
                    xld0=256, xld1=256, sq=32, lnst=32, rst=32, ln=32,
                    lnw0=256, lnw1=256, axg0=32, axg1=32, avv=32, acd=32,
                    ace=32, au=32, asm=32, acs=32, acse=32, aca=32, anum=32,
                    atr=32, aovT=32, ao=32, axl0=256, axl1=256, av1=32,
                    aw0=256, aw1=256)

    def vsp(v):
        v.tensor_copy(junk[:, 0:1], junk[:, 4:5])
        v.tensor_copy(junk[:, 1:2], junk[:, 5:6])

    with nc.Block("init") as b:
        @b.gpsimd
        def _(g):
            g.dma_start(out=routsb[:], in_=router_d.rearrange("(dc p) e -> p dc e", p=128)).then_inc(S("g_ld"), 16)
            g.dma_start(out=k123sb[:], in_=k123_d.rearrange("(dc p) e -> p dc e", p=128)).then_inc(S("g_ld"), 16)
            for e in range(E):
                for m in range(3):
                    g.dma_start(out=wgbf[m, e], in_=gud_d[m][e]).then_inc(S(f"wce{e}"), 16)

        @b.sync
        def _(s):
            for csrc, dstv in ((ident_c, ident), (id16f_c, id16f), (tri_c, tri),
                               (trix_c, trix), (onesc_c, onesc), (onesrf_c, onesr),
                               (iota16_c, iota16), (iotaj_c, iotaj), (widx_c, widx),
                               (ncol_c, ncol), (maskE_c, maskE), (maskO_c, maskO)):
                s.dma_start(out=dstv, in_=csrc).then_inc(S("s_ld"), 16)

        @b.vector
        def _(v):
            v.memset(zeros2k[:], 0.0)
            v.memset(wbase[:], 0.0)
            v.memset(junk[:], 1.0).then_inc(S("v_init"), 1)

    def emit_ln(t, src, prev, xtb, lnob, sqb):
        with nc.Block(t + "lnB") as b:
            @b.sync
            def _(s):
                for pr in prev:
                    s.wait_ge(*pr)
                s.wait_ge(S("s_ld"), 192)
                for i in range(NT):
                    if i >= 2:
                        s.wait_ge(S("sq"), B("sq") + (i - 1))
                        s.wait_ge(S("ln"), B("ln") + (i - 1))
                    s.dma_start(out=xtb[:, i % 2, :], in_=src[i * 128:(i + 1) * 128, :]).then_inc(S(f"xld{i % 2}"), 16)

            @b.scalar
            def _(sc):
                for pr in prev:
                    sc.wait_ge(*pr)
                for i in range(NT):
                    sc.wait_ge(S(f"xld{i % 2}"), B(f"xld{i % 2}") + (16 * (i // 2 + 1)))
                    sc.activation(sqb[:], xtb[:, i % 2, :], AF.Square, bias=0.0,
                                  scale=1.0).then_inc(S("sq"), 1)
                    sc.wait_ge(S("lnst"), B("lnst") + (i + 1))
                    sc.activation(ln_m[:, 4:5], ln_m[:, 3:4], AF.Sqrt, bias=0.0,
                                  scale=1.0).then_inc(S("rst"), 1)
                    sc.wait_ge(S("ln"), B("ln") + (i + 1))
                    sc.dma_start(out=xln[i * 128:(i + 1) * 128, :],
                                 in_=lnob[:, i % 2, :]).then_inc(S(f"lnw{i % 2}"), 16)

            @b.vector
            def _(v):
                for i in range(NT):
                    v.wait_ge(S(f"xld{i % 2}"), B(f"xld{i % 2}") + (16 * (i // 2 + 1)))
                    xt = xtb[:, i % 2, :]
                    v.reduce_sum(ln_m[:, 0:1], xt, axis=AX)
                    v.wait_ge(S("sq"), B("sq") + (i + 1))
                    v.reduce_sum(ln_m[:, 1:2], sqb[:], axis=AX)
                    v.tensor_scalar(ln_m[:, 2:3], ln_m[:, 0:1], 1.0 / D, None, AL.mult)
                    v.tensor_scalar(ln_m[:, 5:6], ln_m[:, 1:2], 1.0 / D, None, AL.mult)
                    vsp(v)
                    v.scalar_tensor_tensor(ln_m[:, 1:2], ln_m[:, 2:3], ln_m[:, 2:3],
                                           ln_m[:, 5:6], AL.mult, AL.subtract)
                    vsp(v)
                    v.tensor_scalar(ln_m[:, 5:6], ln_m[:, 1:2], -1.0, 1e-5, AL.mult, AL.add)
                    vsp(v)
                    v.reciprocal(ln_m[:, 3:4], ln_m[:, 5:6]).then_inc(S("lnst"), 1)
                    if i >= 2:
                        v.wait_ge(S(f"lnw{i % 2}"), B(f"lnw{i % 2}") + (16 * (i // 2)))
                    v.wait_ge(S("rst"), B("rst") + (i + 1))
                    v.tensor_scalar(lnob[:, i % 2, :], xt, ln_m[:, 2:3], ln_m[:, 4:5],
                                    AL.subtract, AL.mult).then_inc(S("ln"), 1)

    def emit_ffn(tag, src, dst, prev, call):
        t = tag

        def _trunc_prev(ph):
            ends = {1: [("lnw0", 256), ("lnw1", 256), ("ln", 32)],
                    2: [("xg0", 32), ("xg1", 32), ("lg", 4), ("zero", 640)],
                    3: [("top", 32)],
                    4: [("tt", 1), ("tr", 224), ("qd", 1)],
                    5: [("obw0", 640), ("obw1", 640), ("sc0", 64), ("sc1", 64), ("wgu0", 1280), ("wgu1", 1280), ("wd0", 640), ("wd1", 640)]}
            return [(S(n), B(n) + v) for n, v in ends[ph]]

        emit_ln(t, src, prev, xt2, lnout, sq_sb)

        if PHASES < 2:
            _pv = _trunc_prev(1)
            adv(FFN_TOT)
            return _pv
        with nc.Block(t + "rt") as b:
            @b.gpsimd
            def _(g):
                for pr in prev:
                    g.wait_ge(*pr)
                g.wait_ge(S("v_init"), 1)
                for r in range(0, NPOS, 256):
                    g.dma_start(
                        out=xe_s[r:r + 256, :].rearrange("a d -> (a d)").rearrange("(p q) -> p q", p=128),
                        in_=zeros2k[:]).then_inc(S("zero"), 16)
                for c in range(4):
                    for par in range(2):
                        g.wait_ge(S(f"lnw{par}"), B(f"lnw{par}") + (16 * 4 * (c + 1)))
                    if c >= 2:
                        g.wait_ge(S("rlg"), B("rlg") + (c - 1))
                    g.dma_gather(xeT[:, c % 2, :, :], xln, widx[:, 64 * c:64 * (c + 1)],
                                 1024, 1024, D, transpose=True, single_packet=False).then_inc(S(f"xg{c % 2}"), 16)

            @b.tensor
            def _(tt_):
                tt_.wait_ge(S("g_ld"), 32)
                for c in range(4):
                    tt_.wait_ge(S(f"xg{c % 2}"), B(f"xg{c % 2}") + (16 * (c // 2 + 1)))
                    if c >= 1:
                        tt_.wait_ge(S("rev"), B("rev") + (c))
                    for dc in range(8):
                        tt_.matmul(pA[0:16, 0:512], routsb[:, dc, :],
                                   xeT[:, c % 2, dc, 0:512], start=(dc == 0), stop=(dc == 7))
                    for dc in range(8):
                        ins = tt_.matmul(pA[0:16, 512:1024], routsb[:, dc, :],
                                         xeT[:, c % 2, dc, 512:1024], start=(dc == 0),
                                         stop=(dc == 7))
                    ins.then_inc(S("rlg"), 1)
                    tt_.wait_ge(S("rev"), B("rev") + (c + 1))
                    if c >= 1:
                        tt_.wait_ge(S("lg"), B("lg") + (c))
                    for j in range(8):
                        ins = tt_.transpose(pD[:, 64 * j:64 * j + 16],
                                            lT_sb[0:16, 128 * j:128 * (j + 1)], id16f[:])
                    ins.then_inc(S("rtr"), 1)

            @b.vector
            def _(v):
                for c in range(4):
                    v.wait_ge(S("rlg"), B("rlg") + (c + 1))
                    if c >= 1:
                        v.wait_ge(S("rtr"), B("rtr") + (c))
                    v.tensor_copy(lT_sb[0:16, :], pA[0:16, :]).then_inc(S("rev"), 1)
                    v.wait_ge(S("rtr"), B("rtr") + (c + 1))
                    v.tensor_copy(logits[:, 8 * c:8 * (c + 1), :],
                                  pD[:].rearrange("p (a bq) -> p a bq", a=8)[:, :, 0:16]).then_inc(S("lg"), 1)

        if PHASES < 3:
            _pv = _trunc_prev(2)
            adv(FFN_TOT)
            return _pv
        with nc.Block(t + "top") as b:
            @b.vector
            def _(v):
                sc0 = silu_sb
                for i in range(NT):
                    v.wait_ge(S("lg"), B("lg") + (i // 8 + 1))
                    lt = logits[:, i, :]
                    v.reduce_max(ln_m[:, 0:1], lt, axis=AX)
                    vsp(v)
                    v.tensor_scalar(ln_m[:, 1:2], ln_m[:, 0:1], -1.0, None,
                                    AL.mult).then_inc(S("tmx"), 1)
                    v.wait_ge(S("texp"), B("texp") + (i + 1))
                    v.reduce_sum(ln_m[:, 2:3], sc0[:, 0:16], axis=AX)
                    vsp(v)
                    v.reciprocal(ln_m[:, 3:4], ln_m[:, 2:3])
                    v.tensor_scalar(sc0[:, 16:32], lt, ln_m[:, 0:1], None, AL.is_ge)
                    vsp(v)
                    v.tensor_tensor_scan(sc0[:, 32:48], sc0[:, 16:32], sc0[:, 16:32],
                                         0.0, AL.add, AL.bypass)
                    vsp(v)
                    v.tensor_scalar(sc0[:, 48:64], sc0[:, 32:48], 1.0, None, AL.is_le)
                    vsp(v)
                    v.tensor_mul(sc0[:, 96:112], sc0[:, 16:32], sc0[:, 48:64])
                    vsp(v)
                    v.tensor_mul(sc0[:, 32:48], sc0[:, 96:112], iota16[:])
                    vsp(v)
                    v.reduce_sum(i1a[:, i:i + 1], sc0[:, 32:48], axis=AX)
                    v.scalar_tensor_tensor(sc0[:, 48:64], sc0[:, 96:112], -1e30, lt,
                                           AL.mult, AL.add)
                    vsp(v)
                    v.reduce_max(ln_m[:, 2:3], sc0[:, 48:64], axis=AX)
                    vsp(v)
                    v.tensor_scalar(sc0[:, 16:32], sc0[:, 48:64], ln_m[:, 2:3], None, AL.is_ge)
                    vsp(v)
                    v.tensor_tensor_scan(sc0[:, 32:48], sc0[:, 16:32], sc0[:, 16:32],
                                         0.0, AL.add, AL.bypass)
                    vsp(v)
                    v.tensor_scalar(sc0[:, 64:80], sc0[:, 32:48], 1.0, None, AL.is_le)
                    vsp(v)
                    v.tensor_mul(sc0[:, 96:112], sc0[:, 16:32], sc0[:, 64:80])
                    vsp(v)
                    v.tensor_mul(sc0[:, 32:48], sc0[:, 96:112], iota16[:])
                    vsp(v)
                    v.reduce_sum(i2a[:, i:i + 1], sc0[:, 32:48], axis=AX)
                    v.tensor_scalar(w1a[:, i:i + 1], ln_m[:, 3:4], 0.5, None, AL.mult)
                    v.tensor_scalar(ln_m[:, 4:5], ln_m[:, 2:3], ln_m[:, 1:2], None,
                                    AL.add).then_inc(S("tm2"), 1)
                    v.wait_ge(S("texp2"), B("texp2") + (i + 1))
                    v.tensor_mul(ln_m[:, 5:6], sc0[:, 80:81], ln_m[:, 3:4])
                    vsp(v)
                    v.tensor_scalar(w2a[:, i:i + 1], ln_m[:, 5:6], 0.5, None,
                                    AL.mult).then_inc(S("top"), 1)

            @b.scalar
            def _(sc):
                for i in range(NT):
                    sc.wait_ge(S("tmx"), B("tmx") + (i + 1))
                    sc.activation(silu_sb[:, 0:16], logits[:, i, :], AF.Exp,
                                  bias=ln_m[:, 1:2], scale=1.0).then_inc(S("texp"), 1)
                    sc.wait_ge(S("tm2"), B("tm2") + (i + 1))
                    sc.activation(silu_sb[:, 80:81], ln_m[:, 4:5], AF.Exp, bias=0.0,
                                  scale=1.0).then_inc(S("texp2"), 1)

        if PHASES < 4:
            _pv = _trunc_prev(3)
            adv(FFN_TOT)
            return _pv
        with nc.Block(t + "pos") as b:
            @b.vector
            def _(v):
                v.wait_ge(S("top"), B("top") + (NT))
                for e in range(E):
                    v.tensor_scalar(eqall[:, 0, e, :], i1a[:], float(e), None, AL.is_equal)
                    v.tensor_scalar(eqall[:, 1, e, :], i2a[:], float(e), None, AL.is_equal)
                vo = v.tensor_copy(junk[:, 0:1], junk[:, 4:5])
                vo.then_inc(S("eq"), 1)
                v.wait_ge(S("cs"), B("cs") + (1))
                v.tensor_copy(rowbuf[0:1, 0:1024], pB[0:1, :])
                vsp(v)
                for k in range(2):
                    for e in range(E):
                        off = k * 512 + e * 32
                        v.tensor_tensor_scan(rowbuf2[0:1, off:off + 32],
                                             rowbuf[0:1, off:off + 32],
                                             rowbuf[0:1, off:off + 32], 0.0, AL.add, AL.bypass)
                vsp(v)
                v.tensor_tensor(rowbuf2[0:1, 1024:2048], rowbuf2[0:1, 0:1024],
                                rowbuf[0:1, 0:1024], AL.subtract)
                v.tensor_copy(rowbuf[0:1, 1024:1056],
                              rowbuf2[0:1, 0:1024].rearrange("p (xx cc) -> p xx cc", cc=32)[:, :, 31])
                vsp(v)
                v.tensor_add(rowbuf[0:1, 1056:1072], rowbuf[0:1, 1024:1040],
                             rowbuf[0:1, 1040:1056])
                vsp(v)
                v.memset(rowbuf[0:1, 1104:1120], 0.0)
                vsp(v)
                acc_a = 1072
                v.memset(rowbuf[0:1, 1072:1088], 0.0)
                vsp(v)
                for m in range(1, NT + 1):
                    acc_b = 1088 if acc_a == 1072 else 1072
                    v.tensor_scalar(rowbuf[0:1, 1104:1120], rowbuf[0:1, 1056:1072],
                                    float(128 * m - 127), None, AL.is_ge)
                    vsp(v)
                    v.tensor_add(rowbuf[0:1, acc_b:acc_b + 16],
                                 rowbuf[0:1, acc_a:acc_a + 16],
                                 rowbuf[0:1, 1104:1120])
                    acc_a = acc_b
                vsp(v)
                v.tensor_copy(rowbuf[0:1, 1104:1120], rowbuf[0:1, acc_a:acc_a + 16])
                vsp(v)
                v.tensor_tensor_scan(rowbuf[0:1, 1152:1168], rowbuf[0:1, 1104:1120],
                                     rowbuf[0:1, 1104:1120], 0.0, AL.add, AL.bypass)
                vsp(v)
                v.tensor_tensor(rowbuf[0:1, 1120:1136], rowbuf[0:1, 1152:1168],
                                rowbuf[0:1, 1104:1120], AL.subtract)
                vsp(v)
                v.tensor_scalar(rowbuf[0:1, 1136:1152], rowbuf[0:1, 1120:1136], 128.0,
                                None, AL.mult)
                vsp(v)
                for k in range(2):
                    for e in range(E):
                        off = k * 512 + e * 32
                        s2 = rowbuf[0:1, 1024 + e:1025 + e] if k == 1 else 0.0
                        v.tensor_scalar(rowbuf2[0:1, off:off + 32],
                                        rowbuf2[0:1, 1024 + off:1024 + off + 32],
                                        rowbuf[0:1, 1136 + e:1137 + e], s2, AL.add, AL.add)
                vo = v.tensor_copy(junk[:, 0:1], junk[:, 4:5])
                vo.then_inc(S("cb"), 1)
                v.wait_ge(S("rk"), B("rk") + (1))
                for k in range(2):
                    pX = pA if k == 0 else pC
                    v.tensor_mul(silu_sb[:, 0:512],
                                 eqall[:, k, :, :].rearrange("p e c -> p (e c)"),
                                 pX[:, 0:512])
                    vsp(v)
                    v.tensor_reduce(qf[:, k, :],
                                    silu_sb[:, 0:512].rearrange("p (e c) -> p c e", e=16),
                                    AX, AL.add)
                    vsp(v)
                v.tensor_copy(q16[:, :, :].rearrange("p a bq -> p (a bq)"),
                              qf[:, :, :].rearrange("p a bq -> p (a bq)")).then_inc(S("qd"), 1)
                v.memset(rowbuf2[0:1, 0:80], 0.0)
                vsp(v)
                ca = 0
                for ee in range(15):
                    v.tensor_scalar(rowbuf2[0:1, 160:240], iotaj[0:1, :],
                                    rowbuf[0:1, 1152 + ee:1153 + ee], None, AL.is_ge)
                    vsp(v)
                    v.tensor_add(rowbuf2[0:1, 80 - ca:160 - ca],
                                 rowbuf2[0:1, ca:ca + 80], rowbuf2[0:1, 160:240])
                    ca = 80 - ca
                vsp(v)
                v.tensor_copy(tiletab[0:1, 160:240], rowbuf2[0:1, ca:ca + 80])
                v.memset(rowbuf2[0:1, 240:242], 1.0)
                v.tensor_tensor(rowbuf2[0:1, 242:320], rowbuf2[0:1, ca + 2:ca + 80],
                                rowbuf2[0:1, ca:ca + 78], AL.subtract)
                vsp(v)
                v.tensor_scalar(rowbuf2[0:1, 242:320], rowbuf2[0:1, 242:320], 0.5,
                                None, AL.is_ge)
                vsp(v)
                v.tensor_copy(tiletab[0:1, 240:320], rowbuf2[0:1, 240:320])
                vo = v.tensor_copy(junk[:, 0:1], junk[:, 4:5])
                vo.then_inc(S("tt"), 1)

            @b.tensor
            def _(tt_):
                tt_.wait_ge(S("eq"), B("eq") + (1))
                eqf = eqall[:].rearrange("p a bq c -> p (a bq c)")
                tt_.matmul(pB[0:1, 0:512], onesc[:], eqf[:, 0:512], start=True, stop=True)
                tt_.matmul(pB[0:1, 512:1024], onesc[:], eqf[:, 512:1024], start=True,
                           stop=True).then_inc(S("cs"), 1)
                tt_.wait_ge(S("cb"), B("cb") + (1))
                tt_.matmul(pA[:, 0:512], trix[:], eqf[:, 0:512], start=True, stop=False)
                tt_.matmul(pA[:, 0:512], onesr[0:1, :], rowbuf2[0:1, 0:512], start=False,
                           stop=True)
                tt_.matmul(pC[:, 0:512], trix[:], eqf[:, 512:1024], start=True, stop=False)
                tt_.matmul(pC[:, 0:512], onesr[0:1, :], rowbuf2[0:1, 512:1024],
                           start=False, stop=True).then_inc(S("rk"), 1)

            @b.sync
            def _(s):
                s.wait_ge(S("qd"), B("qd") + (1))
                for g in range(8):
                    s.dma_start(out=qdram.rearrange("(r k c gg) -> r k c gg", r=16, k=2, c=32)[:, :, :, g],
                                in_=q16[g * 16:(g + 1) * 16, :, :],
                                allow_slow_non_contiguous=True).then_inc(S("tw"), 16)
                for g in range(8):
                    s.dma_start(out=gbdram.rearrange("(k r c gg) -> k r c gg", k=2, r=16, c=32)[0, :, :, g],
                                in_=q16[g * 16:(g + 1) * 16, 0, :],
                                allow_slow_non_contiguous=True).then_inc(S("tw"), 16)
                for g in range(8):
                    s.dma_start(out=gbdram.rearrange("(k r c gg) -> k r c gg", k=2, r=16, c=32)[1, :, :, g],
                                in_=q16[g * 16:(g + 1) * 16, 1, :],
                                allow_slow_non_contiguous=True).then_inc(S("tw"), 16)
                s.wait_ge(S("tw"), B("tw") + (16 * 24))
                s.dma_start(out=scat_idx[0:16, :],
                            in_=qdram.rearrange("(r ss) -> r ss", r=16)).then_inc(S("tr0"), 16)
                s.dma_start(out=gb_idx[0:16, 0:256],
                            in_=gbdram.rearrange("(k r ss) -> k r ss", k=2, r=16)[0]).then_inc(S("tr0"), 16)
                s.dma_start(out=gb_idx[0:16, 256:512],
                            in_=gbdram.rearrange("(k r ss) -> k r ss", k=2, r=16)[1]).then_inc(S("tr0"), 16)
                s.wait_ge(S("tr0"), B("tr0") + (48))
                for g in range(1, 8):
                    s.dma_start(out=scat_idx[16 * g:16 * (g + 1), :],
                                in_=scat_idx[0:16, :]).then_inc(S("tr"), 16)
                    s.dma_start(out=gb_idx[16 * g:16 * (g + 1), :],
                                in_=gb_idx[0:16, :]).then_inc(S("tr"), 16)

        if PHASES < 5:
            _pv = _trunc_prev(4)
            adv(FFN_TOT)
            return _pv
        with nc.Block(t + "xp") as b:
            @b.sync
            def _(s):
                for c in range(4):
                    if c >= 2:
                        s.wait_ge(S(f"sc{c % 2}"), B(f"sc{c % 2}") + (32 * (c // 2)))
                    s.dma_start(out=xeT[:, c % 2, :, :],
                                in_=xln[1024 * c:1024 * (c + 1), :].rearrange(
                                    "(cc p) d -> p cc d", p=128)).then_inc(S(f"dl{c % 2}"), 16)

            @b.gpsimd
            def _(g):
                g.wait_ge(S("tr"), B("tr") + (224))
                g.wait_ge(S("zero"), B("zero") + (16 * 40))
                for c in range(4):
                    g.wait_ge(S(f"dl{c % 2}"), B(f"dl{c % 2}") + (16 * (c // 2 + 1)))
                    for k in range(2):
                        g.dma_scatter_add(xe_s, xeT[:, c % 2, :, :],
                                          scat_idx[:, k * 256 + 64 * c:k * 256 + 64 * c + 64],
                                          1024, 1024, D,
                                          single_packet=False).then_inc(S(f"sc{c % 2}"), 16)
                g.wait_ge(S("sc0"), B("sc0") + (64))
                g.wait_ge(S("sc1"), B("sc1") + (64))
                for gch in range(10):
                    if gch >= 2:
                        g.wait_ge(S("xeu"), B("xeu") + (8 * (gch - 1)))
                    g.dma_gather(xeT[:, gch % 2, :, :], xe_s,
                                 widx[:, 64 * gch:64 * (gch + 1)], 1024, 1024, D,
                                 transpose=True, single_packet=False).then_inc(S(f"gt{gch % 2}"), 16)

            @b.tensor
            def _(tt_):
                tt_.wait_ge(S("tt"), B("tt") + (1))
                for j in range(NTILE):
                    gch = j // 8
                    tt_.wait_ge(S(f"gt{gch % 2}"), B(f"gt{gch % 2}") + (16 * (gch // 2 + 1)))
                    tt_.wait_ge(S(f"wgu{j % 2}"), B(f"wgu{j % 2}") + (32 * (j // 2 + 1)))
                    if j >= 1:
                        tt_.wait_ge(S("ve"), B("ve") + (3 * j))
                        tt_.wait_ge(S("si"), B("si") + (j))
                    li = j % 8
                    xs = xeT[:, (j // 8) % 2, :, :]
                    for dc in range(8):
                        st = xs[:, dc, 128 * li:128 * (li + 1)]
                        wsl = wbuf[:, j % 2, 0, dc, :]
                        tt_.matmul(pA[:, 0:512], st, wsl[:, 0:512], start=(dc == 0), stop=(dc == 7))
                        ins = tt_.matmul(pA[:, 512:1024], st, wsl[:, 512:1024],
                                         start=(dc == 0), stop=(dc == 7))
                    ins.then_inc(S("gd"), 1)
                    for dc in range(8):
                        st = xs[:, dc, 128 * li:128 * (li + 1)]
                        wsl = wbuf[:, j % 2, 1, dc, :]
                        tt_.matmul(pB[:, 0:512], st, wsl[:, 0:512], start=(dc == 0), stop=(dc == 7))
                        ins = tt_.matmul(pB[:, 512:1024], st, wsl[:, 512:1024],
                                         start=(dc == 0), stop=(dc == 7))
                    ins.then_inc(S("xeu"), 1)
                    tt_.wait_ge(S("ve"), B("ve") + (3 * j + 1))
                    for hc in range(8):
                        ins = tt_.transpose(pbf[:, 128 * hc:128 * (hc + 1)],
                                            hid_sb[:, 128 * hc:128 * (hc + 1)], ident[:])
                    ins.then_inc(S("tr2"), 1)
                    tt_.wait_ge(S("ve"), B("ve") + (3 * j + 2))
                    tt_.wait_ge(S(f"wd{j % 2}"), B(f"wd{j % 2}") + (16 * (j // 2 + 1)))
                    for hc in range(8):
                        st = hidT_sb[:, hc, :]
                        wsl = wbuf[:, j % 2, 2, hc, :]
                        tt_.matmul(pC[:, 0:512], st, wsl[:, 0:512], start=(hc == 0), stop=(hc == 7))
                        ins = tt_.matmul(pC[:, 512:1024], st, wsl[:, 512:1024],
                                         start=(hc == 0), stop=(hc == 7))
                    ins.then_inc(S("tl"), 1)

            @b.scalar
            def _(sc):
                def fetch(jj, part):
                    er = sc.value_load(tiletab[0:1, 160 + jj:161 + jj], min_val=0,
                                       max_val=E - 1)
                    fr = sc.value_load(tiletab[0:1, 240 + jj:241 + jj], min_val=0,
                                       max_val=1)
                    ms = range(2) if part == 0 else (2,)
                    nm = "wgu" if part == 0 else "wd"
                    for m in ms:
                        sc.dma_start(out=wbuf[:, jj % 2, m, :, :],
                                     in_=wgbf[m, er].rearrange("(dc p) h -> p dc h", p=128),
                                     cond=fr).then_inc(S(f"{nm}{jj % 2}"), 16)

                sc.wait_ge(S("tt"), B("tt") + (1))
                for e in range(E):
                    sc.wait_ge(S(f"wce{e}"), 48)
                for jj in range(2):
                    fetch(jj, 0)
                    fetch(jj, 1)
                for j in range(NTILE):
                    sc.wait_ge(S("gd"), B("gd") + (j + 1))
                    sc.activation(silu_sb[:], pA[:], AF.Silu, bias=0.0,
                                  scale=1.0).then_inc(S("si"), 1)
                    if j + 2 < NTILE:
                        sc.wait_ge(S("xeu"), B("xeu") + (j + 1))
                        fetch(j + 2, 0)
                        sc.wait_ge(S("tl"), B("tl") + (j + 1))
                        fetch(j + 2, 1)
                    sc.wait_ge(S("ve"), B("ve") + (3 * j + 3))
                    sc.dma_start(out=outs_s[128 * j:128 * (j + 1), :],
                                 in_=ob[:, j % 2, :]).then_inc(S(f"obw{j % 2}"), 16)

            @b.vector
            def _(v):
                for j in range(NTILE):
                    v.wait_ge(S("si"), B("si") + (j + 1))
                    v.wait_ge(S("xeu"), B("xeu") + (j + 1))
                    v.tensor_mul(hid_sb[:], silu_sb[:], pB[:]).then_inc(S("ve"), 1)
                    v.wait_ge(S("tr2"), B("tr2") + (j + 1))
                    v.tensor_copy(hidT_sb[:].rearrange("p a bq -> p (a bq)"),
                                  pbf[:]).then_inc(S("ve"), 1)
                    v.wait_ge(S("tl"), B("tl") + (j + 1))
                    if j >= 2:
                        v.wait_ge(S(f"obw{j % 2}"), B(f"obw{j % 2}") + (16 * (j // 2)))
                    v.tensor_copy(ob[:, j % 2, :], pC[:]).then_inc(S("ve"), 1)

        if PHASES < 6:
            _pv = _trunc_prev(5)
            adv(FFN_TOT)
            return _pv
        with nc.Block(t + "cm") as b:
            @b.gpsimd
            def _(g):
                g.wait_ge(S("obw0"), B("obw0") + (16 * NTILE // 2))
                g.wait_ge(S("obw1"), B("obw1") + (16 * NTILE // 2))
                for c in range(4):
                    if c >= 1:
                        g.wait_ge(S("cmb"), B("cmb") + (8 * c))
                    g.dma_gather(xeT[:, 0, :, :], outs_s, gb_idx[:, 64 * c:64 * c + 64],
                                 1024, 1024, D, single_packet=False).then_inc(S("gb0"), 16)
                    g.dma_gather(xeT[:, 1, :, :], outs_s,
                                 gb_idx[:, 256 + 64 * c:256 + 64 * c + 64],
                                 1024, 1024, D, single_packet=False).then_inc(S("gb1"), 16)

            @b.sync
            def _(s):
                for i in range(NT):
                    if i >= 2:
                        s.wait_ge(S("cmb"), B("cmb") + (i - 1))
                    s.dma_start(out=xt2[:, i % 2, :], in_=src[i * 128:(i + 1) * 128, :]).then_inc(S(f"cx{i % 2}"), 16)

            @b.vector
            def _(v):
                for i in range(NT):
                    c = i // 8
                    v.wait_ge(S("gb0"), B("gb0") + (16 * (c + 1)))
                    v.wait_ge(S("gb1"), B("gb1") + (16 * (c + 1)))
                    v.wait_ge(S(f"cx{i % 2}"), B(f"cx{i % 2}") + (16 * (i // 2 + 1)))
                    if i >= 2:
                        v.wait_ge(S(f"ow{i % 2}"), B(f"ow{i % 2}") + (16 * (i // 2)))
                    li = i % 8
                    v.scalar_tensor_tensor(silu_sb[:],
                                           xeT[:, 0, li, :], w1a[:, i:i + 1],
                                           xt2[:, i % 2, :], AL.mult, AL.add)
                    vsp(v)
                    v.scalar_tensor_tensor(outt[:, i % 2, :], xeT[:, 1, li, :],
                                           w2a[:, i:i + 1], silu_sb[:], AL.mult,
                                           AL.add).then_inc(S("cmb"), 1)

            @b.scalar
            def _(sc):
                for i in range(NT):
                    sc.wait_ge(S("cmb"), B("cmb") + (i + 1))
                    sc.dma_start(out=dst[i * 128:(i + 1) * 128, :],
                                 in_=outt[:, i % 2, :]).then_inc(S(f"ow{i % 2}"), 16)

        adv(FFN_TOT)
        return [(S("ow0"), B("ow0")), (S("ow1"), B("ow1"))]

    def emit_attn(dst, prev):
        t = "a_"

        with nc.Block("a0") as b:
            @b.gpsimd
            def _(g):
                for pr in prev:
                    g.wait_ge(*pr)
                g.dma_start(out=at_wv2[:], in_=wv2_d.rearrange("(dc p) e -> p dc e", p=128)).then_inc(S("wl"), 16)
                g.dma_start(out=at_wo2[:], in_=wo2_d.rearrange("(dc p) e -> p dc e", p=128)).then_inc(S("wl"), 16)

            @b.sync
            def _(s):
                for pr in prev:
                    s.wait_ge(*pr)
                s.dma_start(out=at_abcsb[0:1, :],
                            in_=abc_d.rearrange("a d -> (a d)").rearrange("(o xx) -> o xx", o=1)).then_inc(S("wlh"), 16)

            @b.tensor
            def _(tt_):
                tt_.wait_ge(S("wl"), B("wl") + (32))
                tt_.wait_ge(S("wlh"), B("wlh") + (16))
                for r in range(5):
                    if r >= 1:
                        tt_.wait_ge(S("bce"), B("bce") + (r))
                    tt_.matmul(pA[:, 0:512], onesr[0:1, :],
                               at_abcsb[0:1, 1024 * r:1024 * r + 512], start=True, stop=True)
                    tt_.matmul(pA[:, 512:1024], onesr[0:1, :],
                               at_abcsb[0:1, 1024 * r + 512:1024 * (r + 1)],
                               start=True, stop=True).then_inc(S("bcm"), 1)

            @b.vector
            def _(v):
                for r in range(5):
                    v.wait_ge(S("bcm"), B("bcm") + (r + 1))
                    v.tensor_copy(at_abc[:, r, :], pA[:]).then_inc(S("bce"), 1)
                for pt in range(NSEQT):
                    v.scalar_tensor_tensor(at_sarg[:], at_abc[:, 0, :], ncol[:, pt:pt + 1],
                                           at_abc[:, 1, :], AL.mult, AL.add).then_inc(S("varg"), 1)
                    v.wait_ge(S("asin"), B("asin") + (2 * pt + 1))
                    v.tensor_mul(at_sarg[:], at_sinv[:], at_abc[:, 4, :])
                    vsp(v)
                    v.tensor_reduce(p1a[:, pt, :],
                                    at_sarg[:].rearrange("p (h q) -> p h q", h=H), AX, AL.add)
                    v.scalar_tensor_tensor(at_sarg[:], at_abc[:, 2, :], ncol[:, pt:pt + 1],
                                           at_abc[:, 3, :], AL.mult, AL.add).then_inc(S("varg"), 1)
                    v.wait_ge(S("asin"), B("asin") + (2 * pt + 2))
                    v.tensor_mul(at_sarg[:], at_sinv[:], at_abc[:, 4, :])
                    vsp(v)
                    v.tensor_reduce(at_p2r[:, pt, :],
                                    at_sarg[:].rearrange("p (h q) -> p h q", h=H), AX, AL.add)
                    vsp(v)
                    v.tensor_scalar(at_p2r[:, pt, :], at_p2r[:, pt, :], 20.0, -20.0,
                                    AL.min, AL.max).then_inc(S("vp2"), 1)
                v.memset(carry[0:1, :], 0.0).then_inc(S("car"), 1)

            @b.scalar
            def _(sc):
                for pt in range(NSEQT):
                    sc.wait_ge(S("varg"), B("varg") + (2 * pt + 1))
                    sc.activation(at_sinv[:], at_sarg[:], AF.Sin, bias=0.0,
                                  scale=1.0).then_inc(S("asin"), 1)
                    sc.wait_ge(S("varg"), B("varg") + (2 * pt + 2))
                    sc.activation(at_sinv[:], at_sarg[:], AF.Sin, bias=0.0,
                                  scale=1.0).then_inc(S("asin"), 1)
                sc.wait_ge(S("vp2"), B("vp2") + (NSEQT))
                sc.activation(p2ea[:].rearrange("p a bq -> p (a bq)"),
                              at_p2r[:].rearrange("p a bq -> p (a bq)"), AF.Exp,
                              bias=0.0, scale=1.0).then_inc(S("p2e"), 1)

        emit_ln(t, xres, prev, at_xt, at_lnout, at_sq)

        with nc.Block("a2") as b:
            @b.gpsimd
            def _(g):
                for gc in range(4):
                    for par in range(2):
                        g.wait_ge(S(f"lnw{par}"), B(f"lnw{par}") + (16 * (4 * (gc + 1))))
                    if gc >= 2:
                        g.wait_ge(S("avv"), B("avv") + (8 * (gc - 1)))
                    g.dma_gather(at_xlnT[:, gc % 2, :, :], xln,
                                 widx[:, 64 * gc:64 * (gc + 1)],
                                 1024, 1024, D, transpose=True, single_packet=False).then_inc(S(f"axg{gc % 2}"), 16)

            @b.tensor
            def _(tt_):
                tt_.wait_ge(S("p2e"), B("p2e") + (1))
                tt_.wait_ge(S("car"), B("car") + (1))
                for ti in range(2 * NSEQT):
                    gc = ti // 8
                    li = ti % 8
                    tt_.wait_ge(S(f"axg{gc % 2}"), B(f"axg{gc % 2}") + (16 * (gc // 2 + 1)))
                    if ti >= 1:
                        tt_.wait_ge(S("av1"), B("av1") + (ti))
                    if ti == NSEQT:
                        tt_.wait_ge(S("car"), B("car") + (2))
                    xsl = at_xlnT[:, gc % 2, :, :]
                    for dc in range(8):
                        st = xsl[:, dc, 128 * li:128 * (li + 1)]
                        tt_.matmul(pA[:, 0:512], st, at_wv2[:, dc, 0:512],
                                   start=(dc == 0), stop=(dc == 7))
                        tt_.matmul(pA[:, 512:1024], st, at_wv2[:, dc, 512:1024],
                                   start=(dc == 0), stop=(dc == 7))
                        ins = tt_.matmul(pD[:, 0:48], st, k123sb[:, dc, :],
                                         start=(dc == 0), stop=(dc == 7))
                    ins.then_inc(S("avv"), 1)
                    tt_.wait_ge(S("au"), B("au") + (ti + 1))
                    if ti >= 1:
                        tt_.wait_ge(S("anum"), B("anum") + (ti))
                        tt_.wait_ge(S("aca"), B("aca") + (ti))
                    cur = ti % 2
                    cb = 1088 * cur
                    tt_.matmul(pB[:, 0:512], tri[:], at_u[:, 0:512], start=True, stop=False)
                    tt_.matmul(pB[:, 0:512], onesr[0:1, :], carry[0:1, cb:cb + 512],
                               start=False, stop=True)
                    tt_.matmul(pB[:, 512:1024], tri[:], at_u[:, 512:1024], start=True, stop=False)
                    tt_.matmul(pB[:, 512:1024], onesr[0:1, :], carry[0:1, cb + 512:cb + 1024],
                               start=False, stop=True)
                    tt_.matmul(pD[:, 64:80], tri[:], at_ud[:], start=True, stop=False)
                    tt_.matmul(pD[:, 64:80], onesr[0:1, :], carry[0:1, cb + 1024:cb + 1040],
                               start=False, stop=True).then_inc(S("asm"), 1)
                    if ti >= 1:
                        tt_.wait_ge(S("acse"), B("acse") + (ti))
                    tt_.matmul(pC[0:1, 0:512], onesc[:], at_u[:, 0:512], start=True, stop=True)
                    tt_.matmul(pC[0:1, 512:1024], onesc[:], at_u[:, 512:1024], start=True, stop=True)
                    tt_.matmul(pD[0:1, 96:112], onesc[:], at_ud[:], start=True,
                               stop=True).then_inc(S("acs"), 1)
                    tt_.wait_ge(S("anum"), B("anum") + (ti + 1))
                    for hc in range(8):
                        ins = tt_.transpose(pbf[:, 128 * hc:128 * (hc + 1)],
                                            at_outv[:, 128 * hc:128 * (hc + 1)], ident[:])
                    ins.then_inc(S("atr"), 1)
                    tt_.wait_ge(S("aovT"), B("aovT") + (ti + 1))
                    for hc in range(8):
                        st = at_outvT[:, hc, :]
                        tt_.matmul(pA[:, 0:512], st, at_wo2[:, hc, 0:512],
                                   start=(hc == 0), stop=(hc == 7))
                        ins = tt_.matmul(pA[:, 512:1024], st, at_wo2[:, hc, 512:1024],
                                         start=(hc == 0), stop=(hc == 7))
                    ins.then_inc(S("ao"), 1)

            @b.vector
            def _(v):
                for ti in range(2 * NSEQT):
                    s, i = divmod(ti, NSEQT)
                    if ti == NSEQT:
                        v.wait_ge(S("asm"), B("asm") + (NSEQT))
                        v.memset(carry[0:1, :], 0.0).then_inc(S("car"), 1)
                    v.wait_ge(S("avv"), B("avv") + (ti + 1))
                    v.tensor_add(at_cde[:, 0:16], pD[:, 0:16], p1a[:, i, :])
                    v.tensor_copy(at_cde[:, 16:48], pD[:, 16:48])
                    vsp(v)
                    v.tensor_scalar(at_cde2[:], at_cde[:], 20.0, -20.0, AL.min,
                                    AL.max).then_inc(S("acd"), 1)
                    v.wait_ge(S("ace"), B("ace") + (ti + 1))
                    v.tensor_tensor(at_u[:].rearrange("p (h q) -> p h q", h=H),
                                    pA[:].rearrange("p (h q) -> p h q", h=H),
                                    at_eE[:, 0:16].unsqueeze(2).broadcast_to([128, 16, 64]),
                                    AL.mult)
                    v.tensor_tensor(at_tvd[:].rearrange("p (h q) -> p h q", h=H),
                                    pA[:].rearrange("p (h q) -> p h q", h=H),
                                    at_eE[:, 16:32].unsqueeze(2).broadcast_to([128, 16, 64]),
                                    AL.mult)
                    v.tensor_copy(at_ud[:], at_eE[:, 0:16])
                    v.tensor_mul(at_peee[:], p2ea[:, i, :], at_eE[:, 32:48]).then_inc(S("au"), 1)
                    v.wait_ge(S("acs"), B("acs") + (ti + 1))
                    v.tensor_copy(rowbuf2[0:1, 0:1024], pC[0:1, :])
                    v.tensor_copy(rowbuf2[0:1, 1024:1040], pD[0:1, 96:112]).then_inc(S("acse"), 1)
                    vsp(v)
                    cur, nxt = ti % 2, (ti + 1) % 2
                    v.tensor_add(carry[0:1, 1088 * nxt:1088 * nxt + 1040],
                                 carry[0:1, 1088 * cur:1088 * cur + 1040],
                                 rowbuf2[0:1, 0:1040]).then_inc(S("aca"), 1)
                    v.wait_ge(S("asm"), B("asm") + (ti + 1))
                    v.tensor_tensor(at_num[:].rearrange("p (h q) -> p h q", h=H),
                                    pB[:].rearrange("p (h q) -> p h q", h=H),
                                    at_peee[:].unsqueeze(2).broadcast_to([128, 16, 64]),
                                    AL.mult)
                    vsp(v)
                    v.tensor_add(at_num2[:], at_num[:], at_tvd[:])
                    v.tensor_mul(at_den[:], pD[:, 64:80], at_peee[:])
                    vsp(v)
                    v.tensor_add(at_den2[:], at_den[:], at_eE[:, 16:32])
                    vsp(v)
                    v.reciprocal(at_rden[:], at_den2[:])
                    vsp(v)
                    v.tensor_tensor(at_outv[:].rearrange("p (h q) -> p h q", h=H),
                                    at_num2[:].rearrange("p (h q) -> p h q", h=H),
                                    at_rden[:].unsqueeze(2).broadcast_to([128, 16, 64]),
                                    AL.mult).then_inc(S("anum"), 1)
                    v.wait_ge(S("atr"), B("atr") + (ti + 1))
                    v.tensor_copy(at_outvT[:].rearrange("p a bq -> p (a bq)"),
                                  pbf[:]).then_inc(S("aovT"), 1)
                    v.wait_ge(S("ao"), B("ao") + (ti + 1))
                    v.wait_ge(S(f"axl{ti % 2}"), B(f"axl{ti % 2}") + (16 * (ti // 2 + 1)))
                    if ti >= 2:
                        v.wait_ge(S(f"aw{ti % 2}"), B(f"aw{ti % 2}") + (16 * (ti // 2)))
                    v.tensor_add(at_o[:, ti % 2, :], pA[:], at_xt[:, ti % 2, :]).then_inc(S("av1"), 1)

            @b.scalar
            def _(sc):
                for ti in range(2 * NSEQT):
                    sc.wait_ge(S("acd"), B("acd") + (ti + 1))
                    sc.activation(at_eE[:], at_cde2[:], AF.Exp, bias=0.0,
                                  scale=1.0).then_inc(S("ace"), 1)
                    sc.wait_ge(S("av1"), B("av1") + (ti + 1))
                    sc.dma_start(out=dst[128 * ti:128 * (ti + 1), :],
                                 in_=at_o[:, ti % 2, :]).then_inc(S(f"aw{ti % 2}"), 16)

            @b.sync
            def _(s):
                s.wait_ge(S("ln"), B("ln") + (NT))
                for ti in range(2 * NSEQT):
                    if ti >= 2:
                        s.wait_ge(S("av1"), B("av1") + (ti - 1))
                    s.dma_start(out=at_xt[:, ti % 2, :],
                                in_=xres[128 * ti:128 * (ti + 1), :]).then_inc(S(f"axl{ti % 2}"), 16)

        adv(ATTN_TOT)
        return [(S("aw0"), B("aw0")), (S("aw1"), B("aw1"))]

    prev = [(S("v_init"), 1)]
    if stage == 1:
        prev = emit_ffn("f0", x_in, out_d, prev, 0)
    elif stage == 2:
        prev = emit_ffn("f0", x_in, xres, prev, 0)
        prev = emit_attn(out_d, prev)
    else:
        prev = emit_ffn("f0", x_in, xres, prev, 0)
        prev = emit_attn(xres, prev)
        prev = emit_ffn("f1", xres, out_d, prev, 1)

    with nc.Block("fin") as b:
        @b.sync
        def _(s):
            for pr in prev:
                s.wait_ge(*pr)

        @b.gpsimd
        def _(g):
            for pr in prev:
                g.wait_ge(*pr)

    return nc


def kernel(**inputs):
    x = np.ascontiguousarray(np.asarray(inputs["L0_attn_x"], dtype=np.float32))
    k1 = np.asarray(inputs["L0_attn_k1"], np.float32)
    k2 = np.asarray(inputs["L0_attn_k2"], np.float32)
    k3 = np.asarray(inputs["L0_attn_k3"], np.float32)
    a1 = np.asarray(inputs["L0_attn_a1"], np.float32).reshape(-1)
    a2 = np.asarray(inputs["L0_attn_a2"], np.float32).reshape(-1)
    b1 = np.asarray(inputs["L0_attn_b1"], np.float32).reshape(-1)
    b2 = np.asarray(inputs["L0_attn_b2"], np.float32).reshape(-1)
    cc = np.asarray(inputs["L0_attn_c"], np.float32).reshape(-1)
    vw = np.asarray(inputs["L0_attn_value_weight"], np.float32)
    owt = np.asarray(inputs["L0_attn_output_weight"], np.float32)
    router = np.ascontiguousarray(np.asarray(inputs["L0_ffn_router_w"], np.float32))
    gate = np.ascontiguousarray(np.asarray(inputs["L0_ffn_gate_w"], np.float32))
    up = np.ascontiguousarray(np.asarray(inputs["L0_ffn_up_w"], np.float32))
    down = np.ascontiguousarray(np.asarray(inputs["L0_ffn_down_w"], np.float32))

    wv2 = np.ascontiguousarray(vw.transpose(1, 0, 2).reshape(D, D))
    wo2 = np.ascontiguousarray(owt.transpose(0, 2, 1).reshape(D, D))
    k123 = np.ascontiguousarray(np.concatenate([k1.T, k2.T, k3.T], axis=1))
    abc = np.ascontiguousarray(np.stack([a1, b1, a2, b2, cc], axis=0))

    nc = build(STAGE)
    nc.finalize()

    in_maps = []
    for c in range(NCORES):
        xs = np.ascontiguousarray(x[2 * c:2 * c + 2].reshape(T, D))
        in_maps.append({
            "x": xs, "router_w": router, "gate_w": gate, "up_w": up, "down_w": down,
            "wv2": wv2, "wo2": wo2, "k123": k123, "abc": abc,
        })
    res = run_bass_kernel_spmd(nc, in_maps, core_ids=list(range(NCORES)))
    out = np.stack([r["out"].reshape(2, SEQ, D) for r in res.results], axis=0)
    return out.reshape(NCORES * 2, SEQ, D).astype(np.float32)


# version 02
# speedup vs baseline: 1.0081x; 1.0081x over previous
import numpy as np
import ml_dtypes
import concourse.bass as bass
import concourse.bacc as bacc
import concourse.mybir as mybir
from concourse.bass_utils import run_bass_kernel_spmd

F32 = mybir.dt.float32
BF16 = mybir.dt.bfloat16
I16 = mybir.dt.int16
I32 = mybir.dt.int32
BF = ml_dtypes.bfloat16
AL = mybir.AluOpType
AF = mybir.ActivationFunctionType
AX = mybir.AxisListType.X

NCORES = 8
T = 4096
D = 1024
E = 16
H = 16
NT = T // 128
NTILE = 80
NPOS = NTILE * 128
SEQ = 2048
NSEQT = SEQ // 128

STAGE = 3
PHASES = 99


def _bfb(x):
    return np.asarray(x, np.float32).astype(BF).view(np.uint16)


class Arena:
    def __init__(self, nc, name, nbytes):
        self.h = nc.alloc_sbuf_tensor(name, [128, nbytes], mybir.dt.uint8)
        self.base = self.h.ap()
        self.off = 0
        self.nbytes = nbytes

    def take(self, shape, dtype, name=""):
        esz = mybir.dt.size(dtype)
        free = int(np.prod(shape[1:]))
        nb = free * esz
        self.off = (self.off + 63) // 64 * 64
        assert self.off + nb <= self.nbytes, f"arena overflow {name}: {self.off}+{nb}"
        v = self.base[0:shape[0], self.off:self.off + nb].bitcast(dtype)
        self.off += nb
        if len(shape) == 2:
            return v
        spec = " ".join(f"d{i}" for i in range(1, len(shape)))
        kw = {f"d{i}": shape[i] for i in range(1, len(shape))}
        return v.rearrange(f"p ({spec}) -> p {spec}", **kw)


def build(stage, dbg=False):
    nc = bacc.Bacc("TRN2", target_bir_lowering=False, debug=False,
                   detect_race_conditions=False)
    IK = "ExternalOutput" if dbg else "Internal"

    x_in = nc.dram_tensor("x", [T, D], F32, kind="ExternalInput").ap()
    router_d = nc.dram_tensor("router_w", [D, E], F32, kind="ExternalInput").ap()
    gud_d = [nc.dram_tensor(nm, [E, D, D], F32, kind="ExternalInput").ap()
             for nm in ("gate_w", "up_w", "down_w")]
    wv2_d = nc.dram_tensor("wv2", [D, D], F32, kind="ExternalInput").ap()
    wo2_d = nc.dram_tensor("wo2", [D, D], F32, kind="ExternalInput").ap()
    k123_d = nc.dram_tensor("k123", [D, 48], F32, kind="ExternalInput").ap()
    abc_d = nc.dram_tensor("abc", [5, D], F32, kind="ExternalInput").ap()
    out_d = nc.dram_tensor("out", [T, D], F32, kind="ExternalOutput").ap()

    xres = nc.dram_tensor("xres", [T, D], F32, kind="Internal").ap()
    xln = nc.dram_tensor("xln_s", [T, D], BF16, kind=IK).ap()
    xe_s = nc.dram_tensor("xe_s", [NPOS, D], BF16, kind=IK).ap()
    outs_s = nc.dram_tensor("outs_s", [NPOS, D], BF16, kind=IK).ap()
    qdram = nc.dram_tensor("qdram", [2 * T], I16, kind="Internal").ap()
    wgbf = nc.dram_tensor("wgbf", [3, E, D, D], BF16, kind="Internal").ap()
    gbdram = nc.dram_tensor("gbdram", [2 * T], I16, kind=IK).ap()

    ident_c = nc.inline_tensor(_bfb(np.eye(128)), name="c_ident").ap().bitcast(BF16)
    id16f_c = nc.inline_tensor(np.eye(16, dtype=np.float32), name="c_id16f").ap()
    tri_c = nc.inline_tensor(_bfb(np.triu(np.ones((128, 128)))), name="c_tri").ap().bitcast(BF16)
    trix_c = nc.inline_tensor(_bfb(np.triu(np.ones((128, 128)), 1)), name="c_trix").ap().bitcast(BF16)
    onesc_c = nc.inline_tensor(_bfb(np.ones((128, 1))), name="c_onesc").ap().bitcast(BF16)
    onesrf_c = nc.inline_tensor(np.ones((1, 128), np.float32), name="c_onesrf").ap()
    iota16_c = nc.inline_tensor(
        np.broadcast_to(np.arange(16, dtype=np.float32), (128, 16)).copy(), name="c_iota16").ap()
    iotaj_c = nc.inline_tensor(np.arange(NTILE, dtype=np.float32).reshape(1, NTILE),
                               name="c_iotaj").ap()
    wi = np.zeros((128, NPOS // 16), np.int16)
    for j in range(NPOS):
        wi[j % 16, j // 16] = j
    for g in range(1, 8):
        wi[g * 16:(g + 1) * 16] = wi[:16]
    widx_c = nc.inline_tensor(wi, name="c_widx").ap()
    maskE_c = nc.inline_tensor(
        (np.arange(NTILE) % 2 == 0).astype(np.float32).reshape(1, NTILE), name="c_maskE").ap()
    maskO_c = nc.inline_tensor(
        (np.arange(NTILE) % 2 == 1).astype(np.float32).reshape(1, NTILE), name="c_maskO").ap()
    ncol_c = nc.inline_tensor(
        (np.arange(SEQ, dtype=np.float32) / (SEQ - 1)).reshape(NSEQT, 128).T.copy(),
        name="c_ncol").ap()

    outer = Arena(nc, "outer", 44 * 1024)
    ident = outer.take([128, 128], BF16, "ident")
    id16f = outer.take([16, 16], F32, "id16f")
    tri = outer.take([128, 128], BF16, "tri")
    trix = outer.take([128, 128], BF16, "trix")
    onesc = outer.take([128, 1], BF16, "onesc")
    onesr = outer.take([1, 128], F32, "onesr")
    iota16 = outer.take([128, 16], F32, "iota16")
    iotaj = outer.take([1, NTILE], F32, "iotaj")
    maskE = outer.take([1, NTILE], F32, "maskE")
    maskO = outer.take([1, NTILE], F32, "maskO")
    wbase = outer.take([1, 8], F32, "wbase")
    jmark = outer.take([128, 8], BF16, "jmark")
    widx = outer.take([128, NPOS // 16], I16, "widx")
    ncol = outer.take([128, NSEQT], F32, "ncol")
    routsb = outer.take([128, 8, E], BF16, "routsb")
    k123sb = outer.take([128, 8, 48], BF16, "k123sb")
    p1a = outer.take([128, NSEQT, H], F32, "p1a")
    p2ea = outer.take([128, NSEQT, H], F32, "p2ea")
    zeros2k = outer.take([128, 2048], BF16, "zeros2k")
    junk = outer.take([128, 8], F32, "junk")
    w1a = outer.take([128, NT], F32, "w1a")
    w2a = outer.take([128, NT], F32, "w2a")
    i1a = outer.take([128, NT], F32, "i1a")
    i2a = outer.take([128, NT], F32, "i2a")
    eqall = outer.take([128, 2, E, NT], BF16, "eqall")
    qf = outer.take([128, 2, NT], F32, "qf")
    q16 = outer.take([128, 2, NT], I16, "q16")
    scat_idx = outer.take([128, 512], I16, "scat_idx")
    gb_idx = outer.take([128, 512], I16, "gb_idx")
    rowbuf = outer.take([1, 2048], F32, "rowbuf")
    rowbuf2 = outer.take([1, 2048], F32, "rowbuf2")
    tiletab = outer.take([1, 640], I32, "tiletab")
    carry = outer.take([1, 2176], F32, "carry")
    ln_m = outer.take([128, 6], F32, "ln_m")

    main = Arena(nc, "main", 158 * 1024)
    wbuf = main.take([128, 2, 3, 8, D], BF16, "wbuf")
    xeT = main.take([128, 2, 8, 1024], BF16, "xeT")
    silu_sb = main.take([128, D], F32, "silu")
    xt2 = main.take([128, 2, D], F32, "xt2")
    lT_sb = main.take([16, 1024], F32, "lT")
    logits = main.take([128, NT, E], F32, "logits")
    r8_mark = main.off
    lnout = main.take([128, 2, D], BF16, "lnout")
    sq_sb = main.take([128, D], F32, "sq")
    hid_sb = main.base[0:128, r8_mark:r8_mark + 2048].bitcast(BF16)
    hidT_sb = main.base[0:128, r8_mark + 2048:r8_mark + 4096].bitcast(BF16).rearrange(
        "p (a b) -> p a b", a=8)
    ob = main.base[0:128, r8_mark + 4096:r8_mark + 8192].bitcast(BF16).rearrange(
        "p (a b) -> p a b", a=2)
    outt = main.base[0:128, r8_mark:r8_mark + 8192].bitcast(F32).rearrange(
        "p (a b) -> p a b", a=2)
    ffn_end = main.off
    main.off = 0
    at_wv2 = main.take([128, 8, D], BF16, "at_wv2")
    at_wo2 = main.take([128, 8, D], BF16, "at_wo2")
    at_xlnT = main.take([128, 2, 8, 1024], BF16, "at_xlnT")
    at_abcsb = main.take([1, 5 * D], F32, "at_abcsb")
    at_abc = main.take([128, 5, D], F32, "at_abc")
    at_u = main.take([128, D], BF16, "at_u")
    at_ud = main.take([128, 16], BF16, "at_ud")
    at_tvd = main.take([128, D], F32, "at_tvd")
    at_eE = main.take([128, 48], F32, "at_eE")
    at_cde = main.take([128, 48], F32, "at_cde")
    at_cde2 = main.take([128, 48], F32, "at_cde2")
    at_peee = main.take([128, 16], F32, "at_peee")
    at_den = main.take([128, 16], F32, "at_den")
    at_den2 = main.take([128, 16], F32, "at_den2")
    at_rden = main.take([128, 16], F32, "at_rden")
    at_num = main.take([128, D], F32, "at_num")
    at_num2 = main.take([128, D], F32, "at_num2")
    at_outv = main.take([128, D], BF16, "at_outv")
    at_outvT = main.take([128, 8, 128], BF16, "at_outvT")
    at_xt = main.take([128, 2, D], F32, "at_xt")
    at_o = main.take([128, 2, D], F32, "at_o")
    at_lnout = main.take([128, 2, D], BF16, "at_lnout")
    at_sq = main.take([128, D], F32, "at_sq")
    at_sarg = main.take([128, D], F32, "at_sarg")
    at_sinv = main.take([128, D], F32, "at_sinv")
    at_p2r = main.take([128, NSEQT, H], F32, "at_p2r")

    pA = nc.alloc_psum_tensor("pA", [128, 1024], F32)
    pB = nc.alloc_psum_tensor("pB", [128, 1024], F32)
    pC = nc.alloc_psum_tensor("pC", [128, 1024], F32)
    pD = nc.alloc_psum_tensor("pD", [128, 512], F32)
    pbf = nc.alloc_psum_tensor("pbf", [128, 1024], BF16)

    sems = {}
    base = {}

    def S(name):
        if name not in sems:
            sems[name] = nc.alloc_semaphore(name)
        return sems[name]

    def B(name):
        return base.get(name, 0)

    def adv(totals):
        for k, vv in totals.items():
            base[k] = base.get(k, 0) + vv

    FFN_TOT = dict(xld0=256, xld1=256, sq=32, lnst=32, rst=32, ln=32,
                   lnw0=256, lnw1=256, zero=640, xg0=32, xg1=32, rlg=4, rev=4,
                   rtr=4, lg=4, tmx=32, texp=32, tm2=32, texp2=32, top=32, eq=1,
                   cs=1, cb=1, rk=1, qd=1, tt=1, tw=384, tr0=48, tr=224,
                   dl0=32, dl1=32, sc0=64, sc1=64, wgu0=1280, wgu1=1280, wd0=640, wd1=640, gt0=80,
                   gt1=80, xeu=80, gd=80, si=80, ve=240, tr2=80, tl=80,
                   obw0=640, obw1=640, gb0=64, gb1=64, cx0=256, cx1=256,
                   cmb=32, ow0=256, ow1=256)
    ATTN_TOT = dict(wl=32, wlh=16, bcm=5, bce=5, varg=32, asin=32, vp2=16, p2e=1, car=2,
                    xld0=256, xld1=256, sq=32, lnst=32, rst=32, ln=32,
                    lnw0=256, lnw1=256, axg0=32, axg1=32, avv=32, acd=32,
                    ace=32, au=32, asm=32, acs=32, acse=32, aca=32, anum=32,
                    atr=32, aovT=32, ao=32, axl0=256, axl1=256, av1=32,
                    aw0=256, aw1=256)

    def vsp(v):
        v.tensor_copy(junk[:, 0:1], junk[:, 4:5])
        v.tensor_copy(junk[:, 1:2], junk[:, 5:6])

    CASTS = [(m, e) for e in range(E) for m in range(3)]

    def emit_casts(g, lo, hi):
        for m, e in CASTS[lo:hi]:
            g.dma_start(out=wgbf[m, e], in_=gud_d[m][e]).then_inc(S(f"wce{e}"), 16)

    with nc.Block("init") as b:
        @b.gpsimd
        def _(g):
            g.dma_start(out=routsb[:], in_=router_d.rearrange("(dc p) e -> p dc e", p=128)).then_inc(S("g_ld"), 16)
            g.dma_start(out=k123sb[:], in_=k123_d.rearrange("(dc p) e -> p dc e", p=128)).then_inc(S("g_ld"), 16)

        @b.sync
        def _(s):
            for csrc, dstv in ((ident_c, ident), (id16f_c, id16f), (tri_c, tri),
                               (trix_c, trix), (onesc_c, onesc), (onesrf_c, onesr),
                               (iota16_c, iota16), (iotaj_c, iotaj), (widx_c, widx),
                               (ncol_c, ncol), (maskE_c, maskE), (maskO_c, maskO)):
                s.dma_start(out=dstv, in_=csrc).then_inc(S("s_ld"), 16)

        @b.vector
        def _(v):
            v.memset(zeros2k[:], 0.0)
            v.memset(wbase[:], 0.0)
            v.memset(junk[:], 1.0).then_inc(S("v_init"), 1)

    def emit_ln(t, src, prev, xtb, lnob, sqb, gx=None):
        with nc.Block(t + "lnB") as b:
            if gx is not None:
                @b.gpsimd
                def _(g):
                    gx(g)

            @b.sync
            def _(s):
                for pr in prev:
                    s.wait_ge(*pr)
                s.wait_ge(S("s_ld"), 192)
                for i in range(NT):
                    if i >= 2:
                        s.wait_ge(S("sq"), B("sq") + (i - 1))
                        s.wait_ge(S("ln"), B("ln") + (i - 1))
                    s.dma_start(out=xtb[:, i % 2, :], in_=src[i * 128:(i + 1) * 128, :]).then_inc(S(f"xld{i % 2}"), 16)

            @b.scalar
            def _(sc):
                for pr in prev:
                    sc.wait_ge(*pr)
                for i in range(NT):
                    sc.wait_ge(S(f"xld{i % 2}"), B(f"xld{i % 2}") + (16 * (i // 2 + 1)))
                    sc.activation(sqb[:], xtb[:, i % 2, :], AF.Square, bias=0.0,
                                  scale=1.0).then_inc(S("sq"), 1)
                    sc.wait_ge(S("lnst"), B("lnst") + (i + 1))
                    sc.activation(ln_m[:, 4:5], ln_m[:, 3:4], AF.Sqrt, bias=0.0,
                                  scale=1.0).then_inc(S("rst"), 1)
                    sc.wait_ge(S("ln"), B("ln") + (i + 1))
                    sc.dma_start(out=xln[i * 128:(i + 1) * 128, :],
                                 in_=lnob[:, i % 2, :]).then_inc(S(f"lnw{i % 2}"), 16)

            @b.vector
            def _(v):
                for i in range(NT):
                    v.wait_ge(S(f"xld{i % 2}"), B(f"xld{i % 2}") + (16 * (i // 2 + 1)))
                    xt = xtb[:, i % 2, :]
                    v.reduce_sum(ln_m[:, 0:1], xt, axis=AX)
                    v.wait_ge(S("sq"), B("sq") + (i + 1))
                    v.reduce_sum(ln_m[:, 1:2], sqb[:], axis=AX)
                    v.tensor_scalar(ln_m[:, 2:3], ln_m[:, 0:1], 1.0 / D, None, AL.mult)
                    v.tensor_scalar(ln_m[:, 5:6], ln_m[:, 1:2], 1.0 / D, None, AL.mult)
                    vsp(v)
                    v.scalar_tensor_tensor(ln_m[:, 1:2], ln_m[:, 2:3], ln_m[:, 2:3],
                                           ln_m[:, 5:6], AL.mult, AL.subtract)
                    vsp(v)
                    v.tensor_scalar(ln_m[:, 5:6], ln_m[:, 1:2], -1.0, 1e-5, AL.mult, AL.add)
                    vsp(v)
                    v.reciprocal(ln_m[:, 3:4], ln_m[:, 5:6]).then_inc(S("lnst"), 1)
                    if i >= 2:
                        v.wait_ge(S(f"lnw{i % 2}"), B(f"lnw{i % 2}") + (16 * (i // 2)))
                    v.wait_ge(S("rst"), B("rst") + (i + 1))
                    v.tensor_scalar(lnob[:, i % 2, :], xt, ln_m[:, 2:3], ln_m[:, 4:5],
                                    AL.subtract, AL.mult).then_inc(S("ln"), 1)

    def emit_ffn(tag, src, dst, prev, call):
        t = tag

        def _trunc_prev(ph):
            ends = {1: [("lnw0", 256), ("lnw1", 256), ("ln", 32)],
                    2: [("xg0", 32), ("xg1", 32), ("lg", 4), ("zero", 640)],
                    3: [("top", 32)],
                    4: [("tt", 1), ("tr", 224), ("qd", 1)],
                    5: [("obw0", 640), ("obw1", 640), ("sc0", 64), ("sc1", 64), ("wgu0", 1280), ("wgu1", 1280), ("wd0", 640), ("wd1", 640)]}
            return [(S(n), B(n) + v) for n, v in ends[ph]]

        emit_ln(t, src, prev, xt2, lnout, sq_sb,
                gx=(lambda g: emit_casts(g, 0, 16)) if call == 0 else None)

        if PHASES < 2:
            _pv = _trunc_prev(1)
            adv(FFN_TOT)
            return _pv
        with nc.Block(t + "rt") as b:
            @b.gpsimd
            def _(g):
                for pr in prev:
                    g.wait_ge(*pr)
                for c in range(4):
                    for par in range(2):
                        g.wait_ge(S(f"lnw{par}"), B(f"lnw{par}") + (16 * 4 * (c + 1)))
                    if c >= 2:
                        g.wait_ge(S("rlg"), B("rlg") + (c - 1))
                    g.dma_gather(xeT[:, c % 2, :, :], xln, widx[:, 64 * c:64 * (c + 1)],
                                 1024, 1024, D, transpose=True, single_packet=False).then_inc(S(f"xg{c % 2}"), 16)
                if call == 0:
                    emit_casts(g, 16, 24)

            @b.sync
            def _(s):
                for pr in prev:
                    s.wait_ge(*pr)
                s.wait_ge(S("v_init"), 1)
                for r in range(0, NPOS, 256):
                    s.dma_start(
                        out=xe_s[r:r + 256, :].rearrange("a d -> (a d)").rearrange("(p q) -> p q", p=128),
                        in_=zeros2k[:]).then_inc(S("zero"), 16)

            @b.tensor
            def _(tt_):
                tt_.wait_ge(S("g_ld"), 32)
                for c in range(4):
                    tt_.wait_ge(S(f"xg{c % 2}"), B(f"xg{c % 2}") + (16 * (c // 2 + 1)))
                    if c >= 1:
                        tt_.wait_ge(S("rev"), B("rev") + (c))
                    for dc in range(8):
                        tt_.matmul(pA[0:16, 0:512], routsb[:, dc, :],
                                   xeT[:, c % 2, dc, 0:512], start=(dc == 0), stop=(dc == 7))
                    for dc in range(8):
                        ins = tt_.matmul(pA[0:16, 512:1024], routsb[:, dc, :],
                                         xeT[:, c % 2, dc, 512:1024], start=(dc == 0),
                                         stop=(dc == 7))
                    ins.then_inc(S("rlg"), 1)
                    tt_.wait_ge(S("rev"), B("rev") + (c + 1))
                    if c >= 1:
                        tt_.wait_ge(S("lg"), B("lg") + (c))
                    for j in range(8):
                        ins = tt_.transpose(pD[:, 64 * j:64 * j + 16],
                                            lT_sb[0:16, 128 * j:128 * (j + 1)], id16f[:])
                    ins.then_inc(S("rtr"), 1)

            @b.vector
            def _(v):
                for c in range(4):
                    v.wait_ge(S("rlg"), B("rlg") + (c + 1))
                    if c >= 1:
                        v.wait_ge(S("rtr"), B("rtr") + (c))
                    v.tensor_copy(lT_sb[0:16, :], pA[0:16, :]).then_inc(S("rev"), 1)
                    v.wait_ge(S("rtr"), B("rtr") + (c + 1))
                    v.tensor_copy(logits[:, 8 * c:8 * (c + 1), :],
                                  pD[:].rearrange("p (a bq) -> p a bq", a=8)[:, :, 0:16]).then_inc(S("lg"), 1)

        if PHASES < 3:
            _pv = _trunc_prev(2)
            adv(FFN_TOT)
            return _pv
        with nc.Block(t + "top") as b:
            if call == 0:
                @b.gpsimd
                def _(g):
                    emit_casts(g, 24, 38)

            @b.vector
            def _(v):
                sc0 = silu_sb
                for i in range(NT):
                    v.wait_ge(S("lg"), B("lg") + (i // 8 + 1))
                    lt = logits[:, i, :]
                    v.reduce_max(ln_m[:, 0:1], lt, axis=AX)
                    vsp(v)
                    v.tensor_scalar(ln_m[:, 1:2], ln_m[:, 0:1], -1.0, None,
                                    AL.mult).then_inc(S("tmx"), 1)
                    v.wait_ge(S("texp"), B("texp") + (i + 1))
                    v.reduce_sum(ln_m[:, 2:3], sc0[:, 0:16], axis=AX)
                    vsp(v)
                    v.reciprocal(ln_m[:, 3:4], ln_m[:, 2:3])
                    v.tensor_scalar(sc0[:, 16:32], lt, ln_m[:, 0:1], None, AL.is_ge)
                    vsp(v)
                    v.tensor_tensor_scan(sc0[:, 32:48], sc0[:, 16:32], sc0[:, 16:32],
                                         0.0, AL.add, AL.bypass)
                    vsp(v)
                    v.tensor_scalar(sc0[:, 48:64], sc0[:, 32:48], 1.0, None, AL.is_le)
                    vsp(v)
                    v.tensor_mul(sc0[:, 96:112], sc0[:, 16:32], sc0[:, 48:64])
                    vsp(v)
                    v.tensor_mul(sc0[:, 32:48], sc0[:, 96:112], iota16[:])
                    vsp(v)
                    v.reduce_sum(i1a[:, i:i + 1], sc0[:, 32:48], axis=AX)
                    v.scalar_tensor_tensor(sc0[:, 48:64], sc0[:, 96:112], -1e30, lt,
                                           AL.mult, AL.add)
                    vsp(v)
                    v.reduce_max(ln_m[:, 2:3], sc0[:, 48:64], axis=AX)
                    vsp(v)
                    v.tensor_scalar(sc0[:, 16:32], sc0[:, 48:64], ln_m[:, 2:3], None, AL.is_ge)
                    vsp(v)
                    v.tensor_tensor_scan(sc0[:, 32:48], sc0[:, 16:32], sc0[:, 16:32],
                                         0.0, AL.add, AL.bypass)
                    vsp(v)
                    v.tensor_scalar(sc0[:, 64:80], sc0[:, 32:48], 1.0, None, AL.is_le)
                    vsp(v)
                    v.tensor_mul(sc0[:, 96:112], sc0[:, 16:32], sc0[:, 64:80])
                    vsp(v)
                    v.tensor_mul(sc0[:, 32:48], sc0[:, 96:112], iota16[:])
                    vsp(v)
                    v.reduce_sum(i2a[:, i:i + 1], sc0[:, 32:48], axis=AX)
                    v.tensor_scalar(w1a[:, i:i + 1], ln_m[:, 3:4], 0.5, None, AL.mult)
                    v.tensor_scalar(ln_m[:, 4:5], ln_m[:, 2:3], ln_m[:, 1:2], None,
                                    AL.add).then_inc(S("tm2"), 1)
                    v.wait_ge(S("texp2"), B("texp2") + (i + 1))
                    v.tensor_mul(ln_m[:, 5:6], sc0[:, 80:81], ln_m[:, 3:4])
                    vsp(v)
                    v.tensor_scalar(w2a[:, i:i + 1], ln_m[:, 5:6], 0.5, None,
                                    AL.mult).then_inc(S("top"), 1)

            @b.scalar
            def _(sc):
                for i in range(NT):
                    sc.wait_ge(S("tmx"), B("tmx") + (i + 1))
                    sc.activation(silu_sb[:, 0:16], logits[:, i, :], AF.Exp,
                                  bias=ln_m[:, 1:2], scale=1.0).then_inc(S("texp"), 1)
                    sc.wait_ge(S("tm2"), B("tm2") + (i + 1))
                    sc.activation(silu_sb[:, 80:81], ln_m[:, 4:5], AF.Exp, bias=0.0,
                                  scale=1.0).then_inc(S("texp2"), 1)

        if PHASES < 4:
            _pv = _trunc_prev(3)
            adv(FFN_TOT)
            return _pv
        with nc.Block(t + "pos") as b:
            if call == 0:
                @b.gpsimd
                def _(g):
                    emit_casts(g, 38, 48)

            @b.vector
            def _(v):
                v.wait_ge(S("top"), B("top") + (NT))
                for e in range(E):
                    v.tensor_scalar(eqall[:, 0, e, :], i1a[:], float(e), None, AL.is_equal)
                    v.tensor_scalar(eqall[:, 1, e, :], i2a[:], float(e), None, AL.is_equal)
                vo = v.tensor_copy(junk[:, 0:1], junk[:, 4:5])
                vo.then_inc(S("eq"), 1)
                v.wait_ge(S("cs"), B("cs") + (1))
                v.tensor_copy(rowbuf[0:1, 0:1024], pB[0:1, :])
                vsp(v)
                for k in range(2):
                    for e in range(E):
                        off = k * 512 + e * 32
                        v.tensor_tensor_scan(rowbuf2[0:1, off:off + 32],
                                             rowbuf[0:1, off:off + 32],
                                             rowbuf[0:1, off:off + 32], 0.0, AL.add, AL.bypass)
                vsp(v)
                v.tensor_tensor(rowbuf2[0:1, 1024:2048], rowbuf2[0:1, 0:1024],
                                rowbuf[0:1, 0:1024], AL.subtract)
                v.tensor_copy(rowbuf[0:1, 1024:1056],
                              rowbuf2[0:1, 0:1024].rearrange("p (xx cc) -> p xx cc", cc=32)[:, :, 31])
                vsp(v)
                v.tensor_add(rowbuf[0:1, 1056:1072], rowbuf[0:1, 1024:1040],
                             rowbuf[0:1, 1040:1056])
                vsp(v)
                v.memset(rowbuf[0:1, 1104:1120], 0.0)
                vsp(v)
                acc_a = 1072
                v.memset(rowbuf[0:1, 1072:1088], 0.0)
                vsp(v)
                for m in range(1, NT + 1):
                    acc_b = 1088 if acc_a == 1072 else 1072
                    v.tensor_scalar(rowbuf[0:1, 1104:1120], rowbuf[0:1, 1056:1072],
                                    float(128 * m - 127), None, AL.is_ge)
                    vsp(v)
                    v.tensor_add(rowbuf[0:1, acc_b:acc_b + 16],
                                 rowbuf[0:1, acc_a:acc_a + 16],
                                 rowbuf[0:1, 1104:1120])
                    acc_a = acc_b
                vsp(v)
                v.tensor_copy(rowbuf[0:1, 1104:1120], rowbuf[0:1, acc_a:acc_a + 16])
                vsp(v)
                v.tensor_tensor_scan(rowbuf[0:1, 1152:1168], rowbuf[0:1, 1104:1120],
                                     rowbuf[0:1, 1104:1120], 0.0, AL.add, AL.bypass)
                vsp(v)
                v.tensor_tensor(rowbuf[0:1, 1120:1136], rowbuf[0:1, 1152:1168],
                                rowbuf[0:1, 1104:1120], AL.subtract)
                vsp(v)
                v.tensor_scalar(rowbuf[0:1, 1136:1152], rowbuf[0:1, 1120:1136], 128.0,
                                None, AL.mult)
                vsp(v)
                for k in range(2):
                    for e in range(E):
                        off = k * 512 + e * 32
                        s2 = rowbuf[0:1, 1024 + e:1025 + e] if k == 1 else 0.0
                        v.tensor_scalar(rowbuf2[0:1, off:off + 32],
                                        rowbuf2[0:1, 1024 + off:1024 + off + 32],
                                        rowbuf[0:1, 1136 + e:1137 + e], s2, AL.add, AL.add)
                vo = v.tensor_copy(junk[:, 0:1], junk[:, 4:5])
                vo.then_inc(S("cb"), 1)
                v.wait_ge(S("rk"), B("rk") + (1))
                for k in range(2):
                    pX = pA if k == 0 else pC
                    v.tensor_mul(silu_sb[:, 0:512],
                                 eqall[:, k, :, :].rearrange("p e c -> p (e c)"),
                                 pX[:, 0:512])
                    vsp(v)
                    v.tensor_reduce(qf[:, k, :],
                                    silu_sb[:, 0:512].rearrange("p (e c) -> p c e", e=16),
                                    AX, AL.add)
                    vsp(v)
                v.tensor_copy(q16[:, :, :].rearrange("p a bq -> p (a bq)"),
                              qf[:, :, :].rearrange("p a bq -> p (a bq)")).then_inc(S("qd"), 1)
                v.memset(rowbuf2[0:1, 0:80], 0.0)
                vsp(v)
                ca = 0
                for ee in range(15):
                    v.tensor_scalar(rowbuf2[0:1, 160:240], iotaj[0:1, :],
                                    rowbuf[0:1, 1152 + ee:1153 + ee], None, AL.is_ge)
                    vsp(v)
                    v.tensor_add(rowbuf2[0:1, 80 - ca:160 - ca],
                                 rowbuf2[0:1, ca:ca + 80], rowbuf2[0:1, 160:240])
                    ca = 80 - ca
                vsp(v)
                v.tensor_copy(tiletab[0:1, 160:240], rowbuf2[0:1, ca:ca + 80])
                v.memset(rowbuf2[0:1, 240:242], 1.0)
                v.tensor_tensor(rowbuf2[0:1, 242:320], rowbuf2[0:1, ca + 2:ca + 80],
                                rowbuf2[0:1, ca:ca + 78], AL.subtract)
                vsp(v)
                v.tensor_scalar(rowbuf2[0:1, 242:320], rowbuf2[0:1, 242:320], 0.5,
                                None, AL.is_ge)
                vsp(v)
                v.tensor_copy(tiletab[0:1, 240:320], rowbuf2[0:1, 240:320])
                vo = v.tensor_copy(junk[:, 0:1], junk[:, 4:5])
                vo.then_inc(S("tt"), 1)

            @b.tensor
            def _(tt_):
                tt_.wait_ge(S("eq"), B("eq") + (1))
                eqf = eqall[:].rearrange("p a bq c -> p (a bq c)")
                tt_.matmul(pB[0:1, 0:512], onesc[:], eqf[:, 0:512], start=True, stop=True)
                tt_.matmul(pB[0:1, 512:1024], onesc[:], eqf[:, 512:1024], start=True,
                           stop=True).then_inc(S("cs"), 1)
                tt_.wait_ge(S("cb"), B("cb") + (1))
                tt_.matmul(pA[:, 0:512], trix[:], eqf[:, 0:512], start=True, stop=False)
                tt_.matmul(pA[:, 0:512], onesr[0:1, :], rowbuf2[0:1, 0:512], start=False,
                           stop=True)
                tt_.matmul(pC[:, 0:512], trix[:], eqf[:, 512:1024], start=True, stop=False)
                tt_.matmul(pC[:, 0:512], onesr[0:1, :], rowbuf2[0:1, 512:1024],
                           start=False, stop=True).then_inc(S("rk"), 1)

            @b.sync
            def _(s):
                s.wait_ge(S("qd"), B("qd") + (1))
                for g in range(8):
                    s.dma_start(out=qdram.rearrange("(r k c gg) -> r k c gg", r=16, k=2, c=32)[:, :, :, g],
                                in_=q16[g * 16:(g + 1) * 16, :, :],
                                allow_slow_non_contiguous=True).then_inc(S("tw"), 16)
                for g in range(8):
                    s.dma_start(out=gbdram.rearrange("(k r c gg) -> k r c gg", k=2, r=16, c=32)[0, :, :, g],
                                in_=q16[g * 16:(g + 1) * 16, 0, :],
                                allow_slow_non_contiguous=True).then_inc(S("tw"), 16)
                for g in range(8):
                    s.dma_start(out=gbdram.rearrange("(k r c gg) -> k r c gg", k=2, r=16, c=32)[1, :, :, g],
                                in_=q16[g * 16:(g + 1) * 16, 1, :],
                                allow_slow_non_contiguous=True).then_inc(S("tw"), 16)
                s.wait_ge(S("tw"), B("tw") + (16 * 24))
                s.dma_start(out=scat_idx[0:16, :],
                            in_=qdram.rearrange("(r ss) -> r ss", r=16)).then_inc(S("tr0"), 16)
                s.dma_start(out=gb_idx[0:16, 0:256],
                            in_=gbdram.rearrange("(k r ss) -> k r ss", k=2, r=16)[0]).then_inc(S("tr0"), 16)
                s.dma_start(out=gb_idx[0:16, 256:512],
                            in_=gbdram.rearrange("(k r ss) -> k r ss", k=2, r=16)[1]).then_inc(S("tr0"), 16)
                s.wait_ge(S("tr0"), B("tr0") + (48))
                for g in range(1, 8):
                    s.dma_start(out=scat_idx[16 * g:16 * (g + 1), :],
                                in_=scat_idx[0:16, :]).then_inc(S("tr"), 16)
                    s.dma_start(out=gb_idx[16 * g:16 * (g + 1), :],
                                in_=gb_idx[0:16, :]).then_inc(S("tr"), 16)

        if PHASES < 5:
            _pv = _trunc_prev(4)
            adv(FFN_TOT)
            return _pv
        with nc.Block(t + "xp") as b:
            @b.sync
            def _(s):
                for c in range(4):
                    if c >= 2:
                        s.wait_ge(S(f"sc{c % 2}"), B(f"sc{c % 2}") + (32 * (c // 2)))
                    s.dma_start(out=xeT[:, c % 2, :, :],
                                in_=xln[1024 * c:1024 * (c + 1), :].rearrange(
                                    "(cc p) d -> p cc d", p=128)).then_inc(S(f"dl{c % 2}"), 16)

            @b.gpsimd
            def _(g):
                g.wait_ge(S("tr"), B("tr") + (224))
                g.wait_ge(S("zero"), B("zero") + (16 * 40))
                for c in range(4):
                    g.wait_ge(S(f"dl{c % 2}"), B(f"dl{c % 2}") + (16 * (c // 2 + 1)))
                    for k in range(2):
                        g.dma_scatter_add(xe_s, xeT[:, c % 2, :, :],
                                          scat_idx[:, k * 256 + 64 * c:k * 256 + 64 * c + 64],
                                          1024, 1024, D,
                                          single_packet=False).then_inc(S(f"sc{c % 2}"), 16)
                g.wait_ge(S("sc0"), B("sc0") + (64))
                g.wait_ge(S("sc1"), B("sc1") + (64))
                for gch in range(10):
                    if gch >= 2:
                        g.wait_ge(S("xeu"), B("xeu") + (8 * (gch - 1)))
                    g.dma_gather(xeT[:, gch % 2, :, :], xe_s,
                                 widx[:, 64 * gch:64 * (gch + 1)], 1024, 1024, D,
                                 transpose=True, single_packet=False).then_inc(S(f"gt{gch % 2}"), 16)

            @b.tensor
            def _(tt_):
                tt_.wait_ge(S("tt"), B("tt") + (1))

                def transp(k):
                    tt_.wait_ge(S("ve"), B("ve") + (3 * k + 1))
                    for hc in range(8):
                        ins = tt_.transpose(pbf[:, 128 * hc:128 * (hc + 1)],
                                            hid_sb[:, 128 * hc:128 * (hc + 1)], ident[:])
                    ins.then_inc(S("tr2"), 1)

                def down(k):
                    tt_.wait_ge(S("ve"), B("ve") + (3 * k + 2))
                    tt_.wait_ge(S(f"wd{k % 2}"), B(f"wd{k % 2}") + (16 * (k // 2 + 1)))
                    for hc in range(8):
                        st = hidT_sb[:, hc, :]
                        wsl = wbuf[:, k % 2, 2, hc, :]
                        tt_.matmul(pC[:, 0:512], st, wsl[:, 0:512], start=(hc == 0), stop=(hc == 7))
                        ins = tt_.matmul(pC[:, 512:1024], st, wsl[:, 512:1024],
                                         start=(hc == 0), stop=(hc == 7))
                    ins.then_inc(S("tl"), 1)

                for j in range(NTILE):
                    gch = j // 8
                    tt_.wait_ge(S(f"gt{gch % 2}"), B(f"gt{gch % 2}") + (16 * (gch // 2 + 1)))
                    tt_.wait_ge(S(f"wgu{j % 2}"), B(f"wgu{j % 2}") + (32 * (j // 2 + 1)))
                    if j >= 1:
                        tt_.wait_ge(S("si"), B("si") + (j))
                    li = j % 8
                    xs = xeT[:, (j // 8) % 2, :, :]
                    for dc in range(8):
                        st = xs[:, dc, 128 * li:128 * (li + 1)]
                        wsl = wbuf[:, j % 2, 0, dc, :]
                        tt_.matmul(pA[:, 0:512], st, wsl[:, 0:512], start=(dc == 0), stop=(dc == 7))
                        ins = tt_.matmul(pA[:, 512:1024], st, wsl[:, 512:1024],
                                         start=(dc == 0), stop=(dc == 7))
                    ins.then_inc(S("gd"), 1)
                    if j >= 1:
                        transp(j - 1)
                    for dc in range(8):
                        st = xs[:, dc, 128 * li:128 * (li + 1)]
                        wsl = wbuf[:, j % 2, 1, dc, :]
                        tt_.matmul(pB[:, 0:512], st, wsl[:, 0:512], start=(dc == 0), stop=(dc == 7))
                        ins = tt_.matmul(pB[:, 512:1024], st, wsl[:, 512:1024],
                                         start=(dc == 0), stop=(dc == 7))
                    ins.then_inc(S("xeu"), 1)
                    if j >= 1:
                        down(j - 1)
                transp(NTILE - 1)
                down(NTILE - 1)

            @b.scalar
            def _(sc):
                def fetch(jj, part):
                    er = sc.value_load(tiletab[0:1, 160 + jj:161 + jj], min_val=0,
                                       max_val=E - 1)
                    fr = sc.value_load(tiletab[0:1, 240 + jj:241 + jj], min_val=0,
                                       max_val=1)
                    ms = range(2) if part == 0 else (2,)
                    nm = "wgu" if part == 0 else "wd"
                    for m in ms:
                        sc.dma_start(out=wbuf[:, jj % 2, m, :, :],
                                     in_=wgbf[m, er].rearrange("(dc p) h -> p dc h", p=128),
                                     cond=fr).then_inc(S(f"{nm}{jj % 2}"), 16)

                sc.wait_ge(S("tt"), B("tt") + (1))
                for e in range(E):
                    sc.wait_ge(S(f"wce{e}"), 48)
                for jj in range(2):
                    fetch(jj, 0)
                    fetch(jj, 1)
                for j in range(NTILE):
                    sc.wait_ge(S("gd"), B("gd") + (j + 1))
                    sc.activation(silu_sb[:], pA[:], AF.Silu, bias=0.0,
                                  scale=1.0).then_inc(S("si"), 1)
                    if j + 2 < NTILE:
                        sc.wait_ge(S("xeu"), B("xeu") + (j + 1))
                        fetch(j + 2, 0)
                        sc.wait_ge(S("tl"), B("tl") + (j + 1))
                        fetch(j + 2, 1)
                    sc.wait_ge(S("ve"), B("ve") + (3 * j + 3))
                    sc.dma_start(out=outs_s[128 * j:128 * (j + 1), :],
                                 in_=ob[:, j % 2, :]).then_inc(S(f"obw{j % 2}"), 16)

            @b.vector
            def _(v):
                for j in range(NTILE):
                    v.wait_ge(S("si"), B("si") + (j + 1))
                    v.wait_ge(S("xeu"), B("xeu") + (j + 1))
                    v.tensor_mul(hid_sb[:], silu_sb[:], pB[:]).then_inc(S("ve"), 1)
                    v.wait_ge(S("tr2"), B("tr2") + (j + 1))
                    v.tensor_copy(hidT_sb[:].rearrange("p a bq -> p (a bq)"),
                                  pbf[:]).then_inc(S("ve"), 1)
                    v.wait_ge(S("tl"), B("tl") + (j + 1))
                    if j >= 2:
                        v.wait_ge(S(f"obw{j % 2}"), B(f"obw{j % 2}") + (16 * (j // 2)))
                    v.tensor_copy(ob[:, j % 2, :], pC[:]).then_inc(S("ve"), 1)

        if PHASES < 6:
            _pv = _trunc_prev(5)
            adv(FFN_TOT)
            return _pv
        with nc.Block(t + "cm") as b:
            @b.gpsimd
            def _(g):
                g.wait_ge(S("obw0"), B("obw0") + (16 * NTILE // 2))
                g.wait_ge(S("obw1"), B("obw1") + (16 * NTILE // 2))
                for c in range(4):
                    if c >= 1:
                        g.wait_ge(S("cmb"), B("cmb") + (8 * c))
                    g.dma_gather(xeT[:, 0, :, :], outs_s, gb_idx[:, 64 * c:64 * c + 64],
                                 1024, 1024, D, single_packet=False).then_inc(S("gb0"), 16)
                    g.dma_gather(xeT[:, 1, :, :], outs_s,
                                 gb_idx[:, 256 + 64 * c:256 + 64 * c + 64],
                                 1024, 1024, D, single_packet=False).then_inc(S("gb1"), 16)

            @b.sync
            def _(s):
                for i in range(NT):
                    if i >= 2:
                        s.wait_ge(S("cmb"), B("cmb") + (i - 1))
                    s.dma_start(out=xt2[:, i % 2, :], in_=src[i * 128:(i + 1) * 128, :]).then_inc(S(f"cx{i % 2}"), 16)

            @b.vector
            def _(v):
                for i in range(NT):
                    c = i // 8
                    v.wait_ge(S("gb0"), B("gb0") + (16 * (c + 1)))
                    v.wait_ge(S("gb1"), B("gb1") + (16 * (c + 1)))
                    v.wait_ge(S(f"cx{i % 2}"), B(f"cx{i % 2}") + (16 * (i // 2 + 1)))
                    if i >= 2:
                        v.wait_ge(S(f"ow{i % 2}"), B(f"ow{i % 2}") + (16 * (i // 2)))
                    li = i % 8
                    v.scalar_tensor_tensor(silu_sb[:],
                                           xeT[:, 0, li, :], w1a[:, i:i + 1],
                                           xt2[:, i % 2, :], AL.mult, AL.add)
                    vsp(v)
                    v.scalar_tensor_tensor(outt[:, i % 2, :], xeT[:, 1, li, :],
                                           w2a[:, i:i + 1], silu_sb[:], AL.mult,
                                           AL.add).then_inc(S("cmb"), 1)

            @b.scalar
            def _(sc):
                for i in range(NT):
                    sc.wait_ge(S("cmb"), B("cmb") + (i + 1))
                    sc.dma_start(out=dst[i * 128:(i + 1) * 128, :],
                                 in_=outt[:, i % 2, :]).then_inc(S(f"ow{i % 2}"), 16)

        adv(FFN_TOT)
        return [(S("ow0"), B("ow0")), (S("ow1"), B("ow1"))]

    def emit_attn(dst, prev):
        t = "a_"

        with nc.Block("a0") as b:
            @b.gpsimd
            def _(g):
                for pr in prev:
                    g.wait_ge(*pr)
                g.dma_start(out=at_wv2[:], in_=wv2_d.rearrange("(dc p) e -> p dc e", p=128)).then_inc(S("wl"), 16)
                g.dma_start(out=at_wo2[:], in_=wo2_d.rearrange("(dc p) e -> p dc e", p=128)).then_inc(S("wl"), 16)

            @b.sync
            def _(s):
                for pr in prev:
                    s.wait_ge(*pr)
                s.dma_start(out=at_abcsb[0:1, :],
                            in_=abc_d.rearrange("a d -> (a d)").rearrange("(o xx) -> o xx", o=1)).then_inc(S("wlh"), 16)

            @b.tensor
            def _(tt_):
                tt_.wait_ge(S("wl"), B("wl") + (32))
                tt_.wait_ge(S("wlh"), B("wlh") + (16))
                for r in range(5):
                    if r >= 1:
                        tt_.wait_ge(S("bce"), B("bce") + (r))
                    tt_.matmul(pA[:, 0:512], onesr[0:1, :],
                               at_abcsb[0:1, 1024 * r:1024 * r + 512], start=True, stop=True)
                    tt_.matmul(pA[:, 512:1024], onesr[0:1, :],
                               at_abcsb[0:1, 1024 * r + 512:1024 * (r + 1)],
                               start=True, stop=True).then_inc(S("bcm"), 1)

            @b.vector
            def _(v):
                for r in range(5):
                    v.wait_ge(S("bcm"), B("bcm") + (r + 1))
                    v.tensor_copy(at_abc[:, r, :], pA[:]).then_inc(S("bce"), 1)
                for pt in range(NSEQT):
                    v.scalar_tensor_tensor(at_sarg[:], at_abc[:, 0, :], ncol[:, pt:pt + 1],
                                           at_abc[:, 1, :], AL.mult, AL.add).then_inc(S("varg"), 1)
                    v.wait_ge(S("asin"), B("asin") + (2 * pt + 1))
                    v.tensor_mul(at_sarg[:], at_sinv[:], at_abc[:, 4, :])
                    vsp(v)
                    v.tensor_reduce(p1a[:, pt, :],
                                    at_sarg[:].rearrange("p (h q) -> p h q", h=H), AX, AL.add)
                    v.scalar_tensor_tensor(at_sarg[:], at_abc[:, 2, :], ncol[:, pt:pt + 1],
                                           at_abc[:, 3, :], AL.mult, AL.add).then_inc(S("varg"), 1)
                    v.wait_ge(S("asin"), B("asin") + (2 * pt + 2))
                    v.tensor_mul(at_sarg[:], at_sinv[:], at_abc[:, 4, :])
                    vsp(v)
                    v.tensor_reduce(at_p2r[:, pt, :],
                                    at_sarg[:].rearrange("p (h q) -> p h q", h=H), AX, AL.add)
                    vsp(v)
                    v.tensor_scalar(at_p2r[:, pt, :], at_p2r[:, pt, :], 20.0, -20.0,
                                    AL.min, AL.max).then_inc(S("vp2"), 1)
                v.memset(carry[0:1, :], 0.0).then_inc(S("car"), 1)

            @b.scalar
            def _(sc):
                for pt in range(NSEQT):
                    sc.wait_ge(S("varg"), B("varg") + (2 * pt + 1))
                    sc.activation(at_sinv[:], at_sarg[:], AF.Sin, bias=0.0,
                                  scale=1.0).then_inc(S("asin"), 1)
                    sc.wait_ge(S("varg"), B("varg") + (2 * pt + 2))
                    sc.activation(at_sinv[:], at_sarg[:], AF.Sin, bias=0.0,
                                  scale=1.0).then_inc(S("asin"), 1)
                sc.wait_ge(S("vp2"), B("vp2") + (NSEQT))
                sc.activation(p2ea[:].rearrange("p a bq -> p (a bq)"),
                              at_p2r[:].rearrange("p a bq -> p (a bq)"), AF.Exp,
                              bias=0.0, scale=1.0).then_inc(S("p2e"), 1)

        emit_ln(t, xres, prev, at_xt, at_lnout, at_sq)

        with nc.Block("a2") as b:
            @b.gpsimd
            def _(g):
                for gc in range(4):
                    for par in range(2):
                        g.wait_ge(S(f"lnw{par}"), B(f"lnw{par}") + (16 * (4 * (gc + 1))))
                    if gc >= 2:
                        g.wait_ge(S("avv"), B("avv") + (8 * (gc - 1)))
                    g.dma_gather(at_xlnT[:, gc % 2, :, :], xln,
                                 widx[:, 64 * gc:64 * (gc + 1)],
                                 1024, 1024, D, transpose=True, single_packet=False).then_inc(S(f"axg{gc % 2}"), 16)

            @b.tensor
            def _(tt_):
                tt_.wait_ge(S("p2e"), B("p2e") + (1))
                tt_.wait_ge(S("car"), B("car") + (1))
                for ti in range(2 * NSEQT):
                    gc = ti // 8
                    li = ti % 8
                    tt_.wait_ge(S(f"axg{gc % 2}"), B(f"axg{gc % 2}") + (16 * (gc // 2 + 1)))
                    if ti >= 1:
                        tt_.wait_ge(S("av1"), B("av1") + (ti))
                    if ti == NSEQT:
                        tt_.wait_ge(S("car"), B("car") + (2))
                    xsl = at_xlnT[:, gc % 2, :, :]
                    for dc in range(8):
                        st = xsl[:, dc, 128 * li:128 * (li + 1)]
                        tt_.matmul(pA[:, 0:512], st, at_wv2[:, dc, 0:512],
                                   start=(dc == 0), stop=(dc == 7))
                        tt_.matmul(pA[:, 512:1024], st, at_wv2[:, dc, 512:1024],
                                   start=(dc == 0), stop=(dc == 7))
                        ins = tt_.matmul(pD[:, 0:48], st, k123sb[:, dc, :],
                                         start=(dc == 0), stop=(dc == 7))
                    ins.then_inc(S("avv"), 1)
                    tt_.wait_ge(S("au"), B("au") + (ti + 1))
                    if ti >= 1:
                        tt_.wait_ge(S("anum"), B("anum") + (ti))
                        tt_.wait_ge(S("aca"), B("aca") + (ti))
                    cur = ti % 2
                    cb = 1088 * cur
                    tt_.matmul(pB[:, 0:512], tri[:], at_u[:, 0:512], start=True, stop=False)
                    tt_.matmul(pB[:, 0:512], onesr[0:1, :], carry[0:1, cb:cb + 512],
                               start=False, stop=True)
                    tt_.matmul(pB[:, 512:1024], tri[:], at_u[:, 512:1024], start=True, stop=False)
                    tt_.matmul(pB[:, 512:1024], onesr[0:1, :], carry[0:1, cb + 512:cb + 1024],
                               start=False, stop=True)
                    tt_.matmul(pD[:, 64:80], tri[:], at_ud[:], start=True, stop=False)
                    tt_.matmul(pD[:, 64:80], onesr[0:1, :], carry[0:1, cb + 1024:cb + 1040],
                               start=False, stop=True).then_inc(S("asm"), 1)
                    if ti >= 1:
                        tt_.wait_ge(S("acse"), B("acse") + (ti))
                    tt_.matmul(pC[0:1, 0:512], onesc[:], at_u[:, 0:512], start=True, stop=True)
                    tt_.matmul(pC[0:1, 512:1024], onesc[:], at_u[:, 512:1024], start=True, stop=True)
                    tt_.matmul(pD[0:1, 96:112], onesc[:], at_ud[:], start=True,
                               stop=True).then_inc(S("acs"), 1)
                    tt_.wait_ge(S("anum"), B("anum") + (ti + 1))
                    for hc in range(8):
                        ins = tt_.transpose(pbf[:, 128 * hc:128 * (hc + 1)],
                                            at_outv[:, 128 * hc:128 * (hc + 1)], ident[:])
                    ins.then_inc(S("atr"), 1)
                    tt_.wait_ge(S("aovT"), B("aovT") + (ti + 1))
                    for hc in range(8):
                        st = at_outvT[:, hc, :]
                        tt_.matmul(pA[:, 0:512], st, at_wo2[:, hc, 0:512],
                                   start=(hc == 0), stop=(hc == 7))
                        ins = tt_.matmul(pA[:, 512:1024], st, at_wo2[:, hc, 512:1024],
                                         start=(hc == 0), stop=(hc == 7))
                    ins.then_inc(S("ao"), 1)

            @b.vector
            def _(v):
                for ti in range(2 * NSEQT):
                    s, i = divmod(ti, NSEQT)
                    if ti == NSEQT:
                        v.wait_ge(S("asm"), B("asm") + (NSEQT))
                        v.memset(carry[0:1, :], 0.0).then_inc(S("car"), 1)
                    v.wait_ge(S("avv"), B("avv") + (ti + 1))
                    v.tensor_add(at_cde[:, 0:16], pD[:, 0:16], p1a[:, i, :])
                    v.tensor_copy(at_cde[:, 16:48], pD[:, 16:48])
                    vsp(v)
                    v.tensor_scalar(at_cde2[:], at_cde[:], 20.0, -20.0, AL.min,
                                    AL.max).then_inc(S("acd"), 1)
                    v.wait_ge(S("ace"), B("ace") + (ti + 1))
                    v.tensor_tensor(at_u[:].rearrange("p (h q) -> p h q", h=H),
                                    pA[:].rearrange("p (h q) -> p h q", h=H),
                                    at_eE[:, 0:16].unsqueeze(2).broadcast_to([128, 16, 64]),
                                    AL.mult)
                    v.tensor_tensor(at_tvd[:].rearrange("p (h q) -> p h q", h=H),
                                    pA[:].rearrange("p (h q) -> p h q", h=H),
                                    at_eE[:, 16:32].unsqueeze(2).broadcast_to([128, 16, 64]),
                                    AL.mult)
                    v.tensor_copy(at_ud[:], at_eE[:, 0:16])
                    v.tensor_mul(at_peee[:], p2ea[:, i, :], at_eE[:, 32:48]).then_inc(S("au"), 1)
                    v.wait_ge(S("acs"), B("acs") + (ti + 1))
                    v.tensor_copy(rowbuf2[0:1, 0:1024], pC[0:1, :])
                    v.tensor_copy(rowbuf2[0:1, 1024:1040], pD[0:1, 96:112]).then_inc(S("acse"), 1)
                    vsp(v)
                    cur, nxt = ti % 2, (ti + 1) % 2
                    v.tensor_add(carry[0:1, 1088 * nxt:1088 * nxt + 1040],
                                 carry[0:1, 1088 * cur:1088 * cur + 1040],
                                 rowbuf2[0:1, 0:1040]).then_inc(S("aca"), 1)
                    v.wait_ge(S("asm"), B("asm") + (ti + 1))
                    v.tensor_tensor(at_num[:].rearrange("p (h q) -> p h q", h=H),
                                    pB[:].rearrange("p (h q) -> p h q", h=H),
                                    at_peee[:].unsqueeze(2).broadcast_to([128, 16, 64]),
                                    AL.mult)
                    vsp(v)
                    v.tensor_add(at_num2[:], at_num[:], at_tvd[:])
                    v.tensor_mul(at_den[:], pD[:, 64:80], at_peee[:])
                    vsp(v)
                    v.tensor_add(at_den2[:], at_den[:], at_eE[:, 16:32])
                    vsp(v)
                    v.reciprocal(at_rden[:], at_den2[:])
                    vsp(v)
                    v.tensor_tensor(at_outv[:].rearrange("p (h q) -> p h q", h=H),
                                    at_num2[:].rearrange("p (h q) -> p h q", h=H),
                                    at_rden[:].unsqueeze(2).broadcast_to([128, 16, 64]),
                                    AL.mult).then_inc(S("anum"), 1)
                    v.wait_ge(S("atr"), B("atr") + (ti + 1))
                    v.tensor_copy(at_outvT[:].rearrange("p a bq -> p (a bq)"),
                                  pbf[:]).then_inc(S("aovT"), 1)
                    v.wait_ge(S("ao"), B("ao") + (ti + 1))
                    v.wait_ge(S(f"axl{ti % 2}"), B(f"axl{ti % 2}") + (16 * (ti // 2 + 1)))
                    if ti >= 2:
                        v.wait_ge(S(f"aw{ti % 2}"), B(f"aw{ti % 2}") + (16 * (ti // 2)))
                    v.tensor_add(at_o[:, ti % 2, :], pA[:], at_xt[:, ti % 2, :]).then_inc(S("av1"), 1)

            @b.scalar
            def _(sc):
                for ti in range(2 * NSEQT):
                    sc.wait_ge(S("acd"), B("acd") + (ti + 1))
                    sc.activation(at_eE[:], at_cde2[:], AF.Exp, bias=0.0,
                                  scale=1.0).then_inc(S("ace"), 1)
                    sc.wait_ge(S("av1"), B("av1") + (ti + 1))
                    sc.dma_start(out=dst[128 * ti:128 * (ti + 1), :],
                                 in_=at_o[:, ti % 2, :]).then_inc(S(f"aw{ti % 2}"), 16)

            @b.sync
            def _(s):
                s.wait_ge(S("ln"), B("ln") + (NT))
                for ti in range(2 * NSEQT):
                    if ti >= 2:
                        s.wait_ge(S("av1"), B("av1") + (ti - 1))
                    s.dma_start(out=at_xt[:, ti % 2, :],
                                in_=xres[128 * ti:128 * (ti + 1), :]).then_inc(S(f"axl{ti % 2}"), 16)

        adv(ATTN_TOT)
        return [(S("aw0"), B("aw0")), (S("aw1"), B("aw1"))]

    prev = [(S("v_init"), 1)]
    if stage == 1:
        prev = emit_ffn("f0", x_in, out_d, prev, 0)
    elif stage == 2:
        prev = emit_ffn("f0", x_in, xres, prev, 0)
        prev = emit_attn(out_d, prev)
    else:
        prev = emit_ffn("f0", x_in, xres, prev, 0)
        prev = emit_attn(xres, prev)
        prev = emit_ffn("f1", xres, out_d, prev, 1)

    with nc.Block("fin") as b:
        @b.sync
        def _(s):
            for pr in prev:
                s.wait_ge(*pr)

        @b.gpsimd
        def _(g):
            for pr in prev:
                g.wait_ge(*pr)

    return nc


def kernel(**inputs):
    x = np.ascontiguousarray(np.asarray(inputs["L0_attn_x"], dtype=np.float32))
    k1 = np.asarray(inputs["L0_attn_k1"], np.float32)
    k2 = np.asarray(inputs["L0_attn_k2"], np.float32)
    k3 = np.asarray(inputs["L0_attn_k3"], np.float32)
    a1 = np.asarray(inputs["L0_attn_a1"], np.float32).reshape(-1)
    a2 = np.asarray(inputs["L0_attn_a2"], np.float32).reshape(-1)
    b1 = np.asarray(inputs["L0_attn_b1"], np.float32).reshape(-1)
    b2 = np.asarray(inputs["L0_attn_b2"], np.float32).reshape(-1)
    cc = np.asarray(inputs["L0_attn_c"], np.float32).reshape(-1)
    vw = np.asarray(inputs["L0_attn_value_weight"], np.float32)
    owt = np.asarray(inputs["L0_attn_output_weight"], np.float32)
    router = np.ascontiguousarray(np.asarray(inputs["L0_ffn_router_w"], np.float32))
    gate = np.ascontiguousarray(np.asarray(inputs["L0_ffn_gate_w"], np.float32))
    up = np.ascontiguousarray(np.asarray(inputs["L0_ffn_up_w"], np.float32))
    down = np.ascontiguousarray(np.asarray(inputs["L0_ffn_down_w"], np.float32))

    wv2 = np.ascontiguousarray(vw.transpose(1, 0, 2).reshape(D, D))
    wo2 = np.ascontiguousarray(owt.transpose(0, 2, 1).reshape(D, D))
    k123 = np.ascontiguousarray(np.concatenate([k1.T, k2.T, k3.T], axis=1))
    abc = np.ascontiguousarray(np.stack([a1, b1, a2, b2, cc], axis=0))

    nc = build(STAGE)
    nc.finalize()

    in_maps = []
    for c in range(NCORES):
        xs = np.ascontiguousarray(x[2 * c:2 * c + 2].reshape(T, D))
        in_maps.append({
            "x": xs, "router_w": router, "gate_w": gate, "up_w": up, "down_w": down,
            "wv2": wv2, "wo2": wo2, "k123": k123, "abc": abc,
        })
    res = run_bass_kernel_spmd(nc, in_maps, core_ids=list(range(NCORES)))
    out = np.stack([r["out"].reshape(2, SEQ, D) for r in res.results], axis=0)
    return out.reshape(NCORES * 2, SEQ, D).astype(np.float32)


# version 03
# speedup vs baseline: 1.0519x; 1.0435x over previous
import numpy as np
import ml_dtypes
import concourse.bass as bass
import concourse.bacc as bacc
import concourse.mybir as mybir
from concourse.bass_utils import run_bass_kernel_spmd

F32 = mybir.dt.float32
BF16 = mybir.dt.bfloat16
I16 = mybir.dt.int16
I32 = mybir.dt.int32
BF = ml_dtypes.bfloat16
AL = mybir.AluOpType
AF = mybir.ActivationFunctionType
AX = mybir.AxisListType.X

NCORES = 8
T = 4096
D = 1024
E = 16
H = 16
NT = T // 128
NTILE = 80
NPOS = NTILE * 128
SEQ = 2048
NSEQT = SEQ // 128

STAGE = 3
PHASES = 99


def _bfb(x):
    return np.asarray(x, np.float32).astype(BF).view(np.uint16)


class Arena:
    def __init__(self, nc, name, nbytes):
        self.h = nc.alloc_sbuf_tensor(name, [128, nbytes], mybir.dt.uint8)
        self.base = self.h.ap()
        self.off = 0
        self.nbytes = nbytes

    def take(self, shape, dtype, name=""):
        esz = mybir.dt.size(dtype)
        free = int(np.prod(shape[1:]))
        nb = free * esz
        self.off = (self.off + 63) // 64 * 64
        assert self.off + nb <= self.nbytes, f"arena overflow {name}: {self.off}+{nb}"
        v = self.base[0:shape[0], self.off:self.off + nb].bitcast(dtype)
        self.off += nb
        if len(shape) == 2:
            return v
        spec = " ".join(f"d{i}" for i in range(1, len(shape)))
        kw = {f"d{i}": shape[i] for i in range(1, len(shape))}
        return v.rearrange(f"p ({spec}) -> p {spec}", **kw)


def build(stage, dbg=False):
    nc = bacc.Bacc("TRN2", target_bir_lowering=False, debug=False,
                   detect_race_conditions=False)
    IK = "ExternalOutput" if dbg else "Internal"

    x_in = nc.dram_tensor("x", [T, D], F32, kind="ExternalInput").ap()
    router_d = nc.dram_tensor("router_w", [D, E], F32, kind="ExternalInput").ap()
    gud_d = [nc.dram_tensor(nm, [E, D, D], F32, kind="ExternalInput").ap()
             for nm in ("gate_w", "up_w", "down_w")]
    wv2_d = nc.dram_tensor("wv2", [D, D], F32, kind="ExternalInput").ap()
    wo2_d = nc.dram_tensor("wo2", [D, D], F32, kind="ExternalInput").ap()
    k123_d = nc.dram_tensor("k123", [D, 48], F32, kind="ExternalInput").ap()
    abc_d = nc.dram_tensor("abc", [5, D], F32, kind="ExternalInput").ap()
    out_d = nc.dram_tensor("out", [T, D], F32, kind="ExternalOutput").ap()

    xres = nc.dram_tensor("xres", [T, D], F32, kind="Internal").ap()
    xln = nc.dram_tensor("xln_s", [T, D], BF16, kind=IK).ap()
    xe_s = nc.dram_tensor("xe_s", [NPOS, D], BF16, kind=IK).ap()
    outs_s = nc.dram_tensor("outs_s", [NPOS, D], BF16, kind=IK).ap()
    qdram = nc.dram_tensor("qdram", [2 * T], I16, kind="Internal").ap()
    wgbf = nc.dram_tensor("wgbf", [3, E, D, D], BF16, kind="Internal").ap()
    gbdram = nc.dram_tensor("gbdram", [2 * T], I16, kind=IK).ap()

    ident_c = nc.inline_tensor(_bfb(np.eye(128)), name="c_ident").ap().bitcast(BF16)
    id16f_c = nc.inline_tensor(np.eye(16, dtype=np.float32), name="c_id16f").ap()
    tri_c = nc.inline_tensor(_bfb(np.triu(np.ones((128, 128)))), name="c_tri").ap().bitcast(BF16)
    trix_c = nc.inline_tensor(_bfb(np.triu(np.ones((128, 128)), 1)), name="c_trix").ap().bitcast(BF16)
    onesc_c = nc.inline_tensor(_bfb(np.ones((128, 1))), name="c_onesc").ap().bitcast(BF16)
    onesrf_c = nc.inline_tensor(np.ones((1, 128), np.float32), name="c_onesrf").ap()
    iota16_c = nc.inline_tensor(
        np.broadcast_to(np.arange(16, dtype=np.float32), (128, 16)).copy(), name="c_iota16").ap()
    iotaj_c = nc.inline_tensor(np.arange(NTILE, dtype=np.float32).reshape(1, NTILE),
                               name="c_iotaj").ap()
    wi = np.zeros((128, NPOS // 16), np.int16)
    for j in range(NPOS):
        wi[j % 16, j // 16] = j
    for g in range(1, 8):
        wi[g * 16:(g + 1) * 16] = wi[:16]
    widx_c = nc.inline_tensor(wi, name="c_widx").ap()
    maskE_c = nc.inline_tensor(
        (np.arange(NTILE) % 2 == 0).astype(np.float32).reshape(1, NTILE), name="c_maskE").ap()
    maskO_c = nc.inline_tensor(
        (np.arange(NTILE) % 2 == 1).astype(np.float32).reshape(1, NTILE), name="c_maskO").ap()
    ncol_c = nc.inline_tensor(
        (np.arange(SEQ, dtype=np.float32) / (SEQ - 1)).reshape(NSEQT, 128).T.copy(),
        name="c_ncol").ap()

    outer = Arena(nc, "outer", 44 * 1024)
    ident = outer.take([128, 128], BF16, "ident")
    id16f = outer.take([16, 16], F32, "id16f")
    tri = outer.take([128, 128], BF16, "tri")
    trix = outer.take([128, 128], BF16, "trix")
    onesc = outer.take([128, 1], BF16, "onesc")
    onesr = outer.take([1, 128], F32, "onesr")
    iota16 = outer.take([128, 16], F32, "iota16")
    iotaj = outer.take([1, NTILE], F32, "iotaj")
    maskE = outer.take([1, NTILE], F32, "maskE")
    maskO = outer.take([1, NTILE], F32, "maskO")
    wbase = outer.take([1, 8], F32, "wbase")
    jmark = outer.take([128, 8], BF16, "jmark")
    widx = outer.take([128, NPOS // 16], I16, "widx")
    ncol = outer.take([128, NSEQT], F32, "ncol")
    routsb = outer.take([128, 8, E], BF16, "routsb")
    k123sb = outer.take([128, 8, 48], BF16, "k123sb")
    p1a = outer.take([128, NSEQT, H], F32, "p1a")
    p2ea = outer.take([128, NSEQT, H], F32, "p2ea")
    zeros2k = outer.take([128, 2048], BF16, "zeros2k")
    junk = outer.take([128, 8], F32, "junk")
    w1a = outer.take([128, NT], F32, "w1a")
    w2a = outer.take([128, NT], F32, "w2a")
    i1a = outer.take([128, NT], F32, "i1a")
    i2a = outer.take([128, NT], F32, "i2a")
    eqall = outer.take([128, 2, E, NT], BF16, "eqall")
    qf = outer.take([128, 2, NT], F32, "qf")
    q16 = outer.take([128, 2, NT], I16, "q16")
    scat_idx = outer.take([128, 512], I16, "scat_idx")
    gb_idx = outer.take([128, 512], I16, "gb_idx")
    rowbuf = outer.take([1, 2048], F32, "rowbuf")
    rowbuf2 = outer.take([1, 2048], F32, "rowbuf2")
    tiletab = outer.take([1, 640], I32, "tiletab")
    carry = outer.take([1, 2176], F32, "carry")
    ln_m = outer.take([128, 6], F32, "ln_m")

    main = Arena(nc, "main", 158 * 1024)
    wbuf = main.take([128, 2, 3, 8, D], BF16, "wbuf")
    xeT = main.take([128, 2, 8, 1024], BF16, "xeT")
    silu_sb = main.take([128, D], F32, "silu")
    xt2 = main.take([128, 2, D], F32, "xt2")
    lT_sb = main.take([16, 1024], F32, "lT")
    logits = main.take([128, NT, E], F32, "logits")
    r8_mark = main.off
    lnout = main.take([128, 2, D], BF16, "lnout")
    sq_sb = main.take([128, D], F32, "sq")
    hid_sb = main.base[0:128, r8_mark:r8_mark + 2048].bitcast(BF16)
    hidT_sb = main.base[0:128, r8_mark + 2048:r8_mark + 4096].bitcast(BF16).rearrange(
        "p (a b) -> p a b", a=8)
    ob = main.base[0:128, r8_mark + 4096:r8_mark + 8192].bitcast(BF16).rearrange(
        "p (a b) -> p a b", a=2)
    outt = main.base[0:128, r8_mark:r8_mark + 8192].bitcast(F32).rearrange(
        "p (a b) -> p a b", a=2)
    ffn_end = main.off
    main.off = 0
    at_wv2 = main.take([128, 8, D], BF16, "at_wv2")
    at_wo2 = main.take([128, 8, D], BF16, "at_wo2")
    at_xlnT = main.take([128, 2, 8, 1024], BF16, "at_xlnT")
    at_abcsb = main.take([1, 5 * D], F32, "at_abcsb")
    at_abc = main.take([128, 5, D], F32, "at_abc")
    at_u = main.take([128, D], BF16, "at_u")
    at_ud = main.take([128, 16], BF16, "at_ud")
    at_tvd = main.take([128, D], F32, "at_tvd")
    at_eE = main.take([128, 48], F32, "at_eE")
    at_cde = main.take([128, 48], F32, "at_cde")
    at_cde2 = main.take([128, 48], F32, "at_cde2")
    at_peee = main.take([128, 16], F32, "at_peee")
    at_den = main.take([128, 16], F32, "at_den")
    at_den2 = main.take([128, 16], F32, "at_den2")
    at_rden = main.take([128, 16], F32, "at_rden")
    at_num = main.take([128, D], F32, "at_num")
    at_num2 = main.take([128, D], F32, "at_num2")
    at_outv = main.take([128, D], BF16, "at_outv")
    at_outvT = main.take([128, 8, 128], BF16, "at_outvT")
    at_xt = main.take([128, 2, D], F32, "at_xt")
    at_o = main.take([128, 2, D], F32, "at_o")
    at_lnout = main.take([128, 2, D], BF16, "at_lnout")
    at_sq = main.take([128, D], F32, "at_sq")
    at_sarg = main.take([128, D], F32, "at_sarg")
    at_sinv = main.take([128, D], F32, "at_sinv")
    at_p2r = main.take([128, NSEQT, H], F32, "at_p2r")

    pA = nc.alloc_psum_tensor("pA", [128, 1024], F32)
    pB = nc.alloc_psum_tensor("pB", [128, 1024], F32)
    pC = nc.alloc_psum_tensor("pC", [128, 1024], F32)
    pD = nc.alloc_psum_tensor("pD", [128, 512], F32)
    pbf = nc.alloc_psum_tensor("pbf", [128, 1024], BF16)

    sems = {}
    base = {}

    def S(name):
        if name not in sems:
            sems[name] = nc.alloc_semaphore(name)
        return sems[name]

    def B(name):
        return base.get(name, 0)

    def adv(totals):
        for k, vv in totals.items():
            base[k] = base.get(k, 0) + vv

    FFN_TOT = dict(xld0=256, xld1=256, sq=32, lnst=32, rst=32, ln=32,
                   lnw0=256, lnw1=256, zero=640, xg0=32, xg1=32, rlg=4, rev=4,
                   rtr=4, lg=4, tmx=32, texp=32, tm2=32, texp2=32, top=32, eq=1,
                   cs=1, cb=1, rk=1, qd=1, tt=1, tw=384, tr0=48, tr=224,
                   dl0=32, dl1=32, sc0=64, sc1=64, wgu0=1280, wgu1=1280, wd0=640, wd1=640, gt0=80,
                   gt1=80, xeu=80, gd=80, si=80, ve=240, tr2=80, tl=80,
                   obw0=640, obw1=640, gb0=64, gb1=64, cx0=256, cx1=256,
                   cmb=32, ow0=256, ow1=256)
    ATTN_TOT = dict(wl=32, wlh=16, bcm=5, bce=5, varg=32, asin=32, vp2=16, p2e=1, car=2,
                    xld0=256, xld1=256, sq=32, lnst=32, rst=32, ln=32,
                    lnw0=256, lnw1=256, axg0=32, axg1=32, avv=32, acd=32,
                    ace=32, au=32, asm=32, acs=32, acse=32, aca=32, anum=32,
                    atr=32, aovT=32, ao=32, axl0=256, axl1=256, av1=32,
                    aw0=256, aw1=256)

    def vsp(v):
        v.tensor_copy(junk[:, 0:1], junk[:, 4:5])
        v.tensor_copy(junk[:, 1:2], junk[:, 5:6])

    CASTS = [(m, e) for e in range(E) for m in range(3)]

    def emit_casts(g, lo, hi):
        for m, e in CASTS[lo:hi]:
            g.dma_start(out=wgbf[m, e], in_=gud_d[m][e]).then_inc(S(f"wce{e}"), 16)

    with nc.Block("init") as b:
        @b.gpsimd
        def _(g):
            g.dma_start(out=routsb[:], in_=router_d.rearrange("(dc p) e -> p dc e", p=128)).then_inc(S("g_ld"), 16)
            g.dma_start(out=k123sb[:], in_=k123_d.rearrange("(dc p) e -> p dc e", p=128)).then_inc(S("g_ld"), 16)

        @b.sync
        def _(s):
            for csrc, dstv in ((ident_c, ident), (id16f_c, id16f), (tri_c, tri),
                               (trix_c, trix), (onesc_c, onesc), (onesrf_c, onesr),
                               (iota16_c, iota16), (iotaj_c, iotaj), (widx_c, widx),
                               (ncol_c, ncol), (maskE_c, maskE), (maskO_c, maskO)):
                s.dma_start(out=dstv, in_=csrc).then_inc(S("s_ld"), 16)

        @b.vector
        def _(v):
            v.memset(zeros2k[:], 0.0)
            v.memset(wbase[:], 0.0)
            v.memset(junk[:], 1.0).then_inc(S("v_init"), 1)

    def emit_ln(t, src, prev, xtb, lnob, sqb, gx=None):
        with nc.Block(t + "lnB") as b:
            if gx is not None:
                @b.gpsimd
                def _(g):
                    gx(g)

            @b.sync
            def _(s):
                for pr in prev:
                    s.wait_ge(*pr)
                s.wait_ge(S("s_ld"), 192)
                for i in range(NT):
                    if i >= 2:
                        s.wait_ge(S("sq"), B("sq") + (i - 1))
                        s.wait_ge(S("ln"), B("ln") + (i - 1))
                    s.dma_start(out=xtb[:, i % 2, :], in_=src[i * 128:(i + 1) * 128, :]).then_inc(S(f"xld{i % 2}"), 16)

            @b.scalar
            def _(sc):
                for pr in prev:
                    sc.wait_ge(*pr)
                for i in range(NT):
                    sc.wait_ge(S(f"xld{i % 2}"), B(f"xld{i % 2}") + (16 * (i // 2 + 1)))
                    sc.activation(sqb[:], xtb[:, i % 2, :], AF.Square, bias=0.0,
                                  scale=1.0).then_inc(S("sq"), 1)
                    sc.wait_ge(S("lnst"), B("lnst") + (i + 1))
                    sc.activation(ln_m[:, 4:5], ln_m[:, 3:4], AF.Sqrt, bias=0.0,
                                  scale=1.0).then_inc(S("rst"), 1)
                    sc.wait_ge(S("ln"), B("ln") + (i + 1))
                    sc.dma_start(out=xln[i * 128:(i + 1) * 128, :],
                                 in_=lnob[:, i % 2, :]).then_inc(S(f"lnw{i % 2}"), 16)

            @b.vector
            def _(v):
                for i in range(NT):
                    v.wait_ge(S(f"xld{i % 2}"), B(f"xld{i % 2}") + (16 * (i // 2 + 1)))
                    xt = xtb[:, i % 2, :]
                    v.reduce_sum(ln_m[:, 0:1], xt, axis=AX)
                    v.wait_ge(S("sq"), B("sq") + (i + 1))
                    v.reduce_sum(ln_m[:, 1:2], sqb[:], axis=AX)
                    v.tensor_scalar(ln_m[:, 2:3], ln_m[:, 0:1], 1.0 / D, None, AL.mult)
                    v.tensor_scalar(ln_m[:, 5:6], ln_m[:, 1:2], 1.0 / D, None, AL.mult)
                    vsp(v)
                    v.scalar_tensor_tensor(ln_m[:, 1:2], ln_m[:, 2:3], ln_m[:, 2:3],
                                           ln_m[:, 5:6], AL.mult, AL.subtract)
                    vsp(v)
                    v.tensor_scalar(ln_m[:, 5:6], ln_m[:, 1:2], -1.0, 1e-5, AL.mult, AL.add)
                    vsp(v)
                    v.reciprocal(ln_m[:, 3:4], ln_m[:, 5:6]).then_inc(S("lnst"), 1)
                    if i >= 2:
                        v.wait_ge(S(f"lnw{i % 2}"), B(f"lnw{i % 2}") + (16 * (i // 2)))
                    v.wait_ge(S("rst"), B("rst") + (i + 1))
                    v.tensor_scalar(lnob[:, i % 2, :], xt, ln_m[:, 2:3], ln_m[:, 4:5],
                                    AL.subtract, AL.mult).then_inc(S("ln"), 1)

    def emit_ffn(tag, src, dst, prev, call):
        t = tag

        def _trunc_prev(ph):
            ends = {1: [("lnw0", 256), ("lnw1", 256), ("ln", 32)],
                    2: [("xg0", 32), ("xg1", 32), ("lg", 4), ("zero", 640)],
                    3: [("top", 32)],
                    4: [("tt", 1), ("tr", 224), ("qd", 1)],
                    5: [("obw0", 640), ("obw1", 640), ("sc0", 64), ("sc1", 64), ("wgu0", 1280), ("wgu1", 1280), ("wd0", 640), ("wd1", 640)]}
            return [(S(n), B(n) + v) for n, v in ends[ph]]

        emit_ln(t, src, prev, xt2, lnout, sq_sb,
                gx=(lambda g: emit_casts(g, 0, 16)) if call == 0 else None)

        if PHASES < 2:
            _pv = _trunc_prev(1)
            adv(FFN_TOT)
            return _pv
        with nc.Block(t + "rt") as b:
            @b.gpsimd
            def _(g):
                for pr in prev:
                    g.wait_ge(*pr)
                for c in range(4):
                    for par in range(2):
                        g.wait_ge(S(f"lnw{par}"), B(f"lnw{par}") + (16 * 4 * (c + 1)))
                    if c >= 2:
                        g.wait_ge(S("rlg"), B("rlg") + (c - 1))
                    g.dma_gather(xeT[:, c % 2, :, :], xln, widx[:, 64 * c:64 * (c + 1)],
                                 1024, 1024, D, transpose=True, single_packet=False).then_inc(S(f"xg{c % 2}"), 16)
                if call == 0:
                    emit_casts(g, 16, 24)

            @b.sync
            def _(s):
                for pr in prev:
                    s.wait_ge(*pr)
                s.wait_ge(S("v_init"), 1)
                for r in range(0, NPOS, 256):
                    s.dma_start(
                        out=xe_s[r:r + 256, :].rearrange("a d -> (a d)").rearrange("(p q) -> p q", p=128),
                        in_=zeros2k[:]).then_inc(S("zero"), 16)

            @b.tensor
            def _(tt_):
                tt_.wait_ge(S("g_ld"), 32)
                for c in range(4):
                    tt_.wait_ge(S(f"xg{c % 2}"), B(f"xg{c % 2}") + (16 * (c // 2 + 1)))
                    if c >= 1:
                        tt_.wait_ge(S("rev"), B("rev") + (c))
                    for dc in range(8):
                        tt_.matmul(pA[0:16, 0:512], routsb[:, dc, :],
                                   xeT[:, c % 2, dc, 0:512], start=(dc == 0), stop=(dc == 7))
                    for dc in range(8):
                        ins = tt_.matmul(pA[0:16, 512:1024], routsb[:, dc, :],
                                         xeT[:, c % 2, dc, 512:1024], start=(dc == 0),
                                         stop=(dc == 7))
                    ins.then_inc(S("rlg"), 1)
                    tt_.wait_ge(S("rev"), B("rev") + (c + 1))
                    if c >= 1:
                        tt_.wait_ge(S("lg"), B("lg") + (c))
                    for j in range(8):
                        ins = tt_.transpose(pD[:, 64 * j:64 * j + 16],
                                            lT_sb[0:16, 128 * j:128 * (j + 1)], id16f[:])
                    ins.then_inc(S("rtr"), 1)

            @b.vector
            def _(v):
                for c in range(4):
                    v.wait_ge(S("rlg"), B("rlg") + (c + 1))
                    if c >= 1:
                        v.wait_ge(S("rtr"), B("rtr") + (c))
                    v.tensor_copy(lT_sb[0:16, :], pA[0:16, :]).then_inc(S("rev"), 1)
                    v.wait_ge(S("rtr"), B("rtr") + (c + 1))
                    v.tensor_copy(logits[:, 8 * c:8 * (c + 1), :],
                                  pD[:].rearrange("p (a bq) -> p a bq", a=8)[:, :, 0:16]).then_inc(S("lg"), 1)

        if PHASES < 3:
            _pv = _trunc_prev(2)
            adv(FFN_TOT)
            return _pv
        with nc.Block(t + "top") as b:
            if call == 0:
                @b.gpsimd
                def _(g):
                    emit_casts(g, 24, 38)

            @b.vector
            def _(v):
                sc0 = silu_sb
                for i in range(NT):
                    v.wait_ge(S("lg"), B("lg") + (i // 8 + 1))
                    lt = logits[:, i, :]
                    v.reduce_max(ln_m[:, 0:1], lt, axis=AX)
                    vsp(v)
                    v.tensor_scalar(ln_m[:, 1:2], ln_m[:, 0:1], -1.0, None,
                                    AL.mult).then_inc(S("tmx"), 1)
                    v.wait_ge(S("texp"), B("texp") + (i + 1))
                    v.reduce_sum(ln_m[:, 2:3], sc0[:, 0:16], axis=AX)
                    vsp(v)
                    v.reciprocal(ln_m[:, 3:4], ln_m[:, 2:3])
                    v.tensor_scalar(sc0[:, 16:32], lt, ln_m[:, 0:1], None, AL.is_ge)
                    vsp(v)
                    v.tensor_tensor_scan(sc0[:, 32:48], sc0[:, 16:32], sc0[:, 16:32],
                                         0.0, AL.add, AL.bypass)
                    vsp(v)
                    v.tensor_scalar(sc0[:, 48:64], sc0[:, 32:48], 1.0, None, AL.is_le)
                    vsp(v)
                    v.tensor_mul(sc0[:, 96:112], sc0[:, 16:32], sc0[:, 48:64])
                    vsp(v)
                    v.tensor_mul(sc0[:, 32:48], sc0[:, 96:112], iota16[:])
                    vsp(v)
                    v.reduce_sum(i1a[:, i:i + 1], sc0[:, 32:48], axis=AX)
                    v.scalar_tensor_tensor(sc0[:, 48:64], sc0[:, 96:112], -1e30, lt,
                                           AL.mult, AL.add)
                    vsp(v)
                    v.reduce_max(ln_m[:, 2:3], sc0[:, 48:64], axis=AX)
                    vsp(v)
                    v.tensor_scalar(sc0[:, 16:32], sc0[:, 48:64], ln_m[:, 2:3], None, AL.is_ge)
                    vsp(v)
                    v.tensor_tensor_scan(sc0[:, 32:48], sc0[:, 16:32], sc0[:, 16:32],
                                         0.0, AL.add, AL.bypass)
                    vsp(v)
                    v.tensor_scalar(sc0[:, 64:80], sc0[:, 32:48], 1.0, None, AL.is_le)
                    vsp(v)
                    v.tensor_mul(sc0[:, 96:112], sc0[:, 16:32], sc0[:, 64:80])
                    vsp(v)
                    v.tensor_mul(sc0[:, 32:48], sc0[:, 96:112], iota16[:])
                    vsp(v)
                    v.reduce_sum(i2a[:, i:i + 1], sc0[:, 32:48], axis=AX)
                    v.tensor_scalar(w1a[:, i:i + 1], ln_m[:, 3:4], 0.5, None, AL.mult)
                    v.tensor_scalar(ln_m[:, 4:5], ln_m[:, 2:3], ln_m[:, 1:2], None,
                                    AL.add).then_inc(S("tm2"), 1)
                    v.wait_ge(S("texp2"), B("texp2") + (i + 1))
                    v.tensor_mul(ln_m[:, 5:6], sc0[:, 80:81], ln_m[:, 3:4])
                    vsp(v)
                    v.tensor_scalar(w2a[:, i:i + 1], ln_m[:, 5:6], 0.5, None,
                                    AL.mult).then_inc(S("top"), 1)

            @b.scalar
            def _(sc):
                for i in range(NT):
                    sc.wait_ge(S("tmx"), B("tmx") + (i + 1))
                    sc.activation(silu_sb[:, 0:16], logits[:, i, :], AF.Exp,
                                  bias=ln_m[:, 1:2], scale=1.0).then_inc(S("texp"), 1)
                    sc.wait_ge(S("tm2"), B("tm2") + (i + 1))
                    sc.activation(silu_sb[:, 80:81], ln_m[:, 4:5], AF.Exp, bias=0.0,
                                  scale=1.0).then_inc(S("texp2"), 1)

        if PHASES < 4:
            _pv = _trunc_prev(3)
            adv(FFN_TOT)
            return _pv
        with nc.Block(t + "pos") as b:
            if call == 0:
                @b.gpsimd
                def _(g):
                    emit_casts(g, 38, 48)

            @b.vector
            def _(v):
                v.wait_ge(S("top"), B("top") + (NT))
                for e in range(E):
                    v.tensor_scalar(eqall[:, 0, e, :], i1a[:], float(e), None, AL.is_equal)
                    v.tensor_scalar(eqall[:, 1, e, :], i2a[:], float(e), None, AL.is_equal)
                vo = v.tensor_copy(junk[:, 0:1], junk[:, 4:5])
                vo.then_inc(S("eq"), 1)
                v.wait_ge(S("cs"), B("cs") + (1))
                v.tensor_copy(rowbuf[0:1, 0:1024], pB[0:1, :])
                vsp(v)
                for k in range(2):
                    for e in range(E):
                        off = k * 512 + e * 32
                        v.tensor_tensor_scan(rowbuf2[0:1, off:off + 32],
                                             rowbuf[0:1, off:off + 32],
                                             rowbuf[0:1, off:off + 32], 0.0, AL.add, AL.bypass)
                vsp(v)
                v.tensor_tensor(rowbuf2[0:1, 1024:2048], rowbuf2[0:1, 0:1024],
                                rowbuf[0:1, 0:1024], AL.subtract)
                v.tensor_copy(rowbuf[0:1, 1024:1056],
                              rowbuf2[0:1, 0:1024].rearrange("p (xx cc) -> p xx cc", cc=32)[:, :, 31])
                vsp(v)
                v.tensor_add(rowbuf[0:1, 1056:1072], rowbuf[0:1, 1024:1040],
                             rowbuf[0:1, 1040:1056])
                vsp(v)
                v.memset(rowbuf[0:1, 1104:1120], 0.0)
                vsp(v)
                acc_a = 1072
                v.memset(rowbuf[0:1, 1072:1088], 0.0)
                vsp(v)
                for m in range(1, NT + 1):
                    acc_b = 1088 if acc_a == 1072 else 1072
                    v.tensor_scalar(rowbuf[0:1, 1104:1120], rowbuf[0:1, 1056:1072],
                                    float(128 * m - 127), None, AL.is_ge)
                    vsp(v)
                    v.tensor_add(rowbuf[0:1, acc_b:acc_b + 16],
                                 rowbuf[0:1, acc_a:acc_a + 16],
                                 rowbuf[0:1, 1104:1120])
                    acc_a = acc_b
                vsp(v)
                v.tensor_copy(rowbuf[0:1, 1104:1120], rowbuf[0:1, acc_a:acc_a + 16])
                vsp(v)
                v.tensor_tensor_scan(rowbuf[0:1, 1152:1168], rowbuf[0:1, 1104:1120],
                                     rowbuf[0:1, 1104:1120], 0.0, AL.add, AL.bypass)
                vsp(v)
                v.tensor_tensor(rowbuf[0:1, 1120:1136], rowbuf[0:1, 1152:1168],
                                rowbuf[0:1, 1104:1120], AL.subtract)
                vsp(v)
                v.tensor_scalar(rowbuf[0:1, 1136:1152], rowbuf[0:1, 1120:1136], 128.0,
                                None, AL.mult)
                vsp(v)
                for k in range(2):
                    for e in range(E):
                        off = k * 512 + e * 32
                        s2 = rowbuf[0:1, 1024 + e:1025 + e] if k == 1 else 0.0
                        v.tensor_scalar(rowbuf2[0:1, off:off + 32],
                                        rowbuf2[0:1, 1024 + off:1024 + off + 32],
                                        rowbuf[0:1, 1136 + e:1137 + e], s2, AL.add, AL.add)
                vo = v.tensor_copy(junk[:, 0:1], junk[:, 4:5])
                vo.then_inc(S("cb"), 1)
                v.wait_ge(S("rk"), B("rk") + (1))
                for k in range(2):
                    pX = pA if k == 0 else pC
                    v.tensor_mul(silu_sb[:, 0:512],
                                 eqall[:, k, :, :].rearrange("p e c -> p (e c)"),
                                 pX[:, 0:512])
                    vsp(v)
                    v.tensor_reduce(qf[:, k, :],
                                    silu_sb[:, 0:512].rearrange("p (e c) -> p c e", e=16),
                                    AX, AL.add)
                    vsp(v)
                v.tensor_copy(q16[:, :, :].rearrange("p a bq -> p (a bq)"),
                              qf[:, :, :].rearrange("p a bq -> p (a bq)")).then_inc(S("qd"), 1)
                v.memset(rowbuf2[0:1, 0:80], 0.0)
                vsp(v)
                ca = 0
                for ee in range(15):
                    v.tensor_scalar(rowbuf2[0:1, 160:240], iotaj[0:1, :],
                                    rowbuf[0:1, 1152 + ee:1153 + ee], None, AL.is_ge)
                    vsp(v)
                    v.tensor_add(rowbuf2[0:1, 80 - ca:160 - ca],
                                 rowbuf2[0:1, ca:ca + 80], rowbuf2[0:1, 160:240])
                    ca = 80 - ca
                vsp(v)
                v.tensor_copy(tiletab[0:1, 160:240], rowbuf2[0:1, ca:ca + 80])
                v.memset(rowbuf2[0:1, 240:242], 1.0)
                v.tensor_tensor(rowbuf2[0:1, 242:320], rowbuf2[0:1, ca + 2:ca + 80],
                                rowbuf2[0:1, ca:ca + 78], AL.subtract)
                vsp(v)
                v.tensor_scalar(rowbuf2[0:1, 242:320], rowbuf2[0:1, 242:320], 0.5,
                                None, AL.is_ge)
                vsp(v)
                v.tensor_copy(tiletab[0:1, 240:320], rowbuf2[0:1, 240:320])
                vo = v.tensor_copy(junk[:, 0:1], junk[:, 4:5])
                vo.then_inc(S("tt"), 1)

            @b.tensor
            def _(tt_):
                tt_.wait_ge(S("eq"), B("eq") + (1))
                eqf = eqall[:].rearrange("p a bq c -> p (a bq c)")
                tt_.matmul(pB[0:1, 0:512], onesc[:], eqf[:, 0:512], start=True, stop=True)
                tt_.matmul(pB[0:1, 512:1024], onesc[:], eqf[:, 512:1024], start=True,
                           stop=True).then_inc(S("cs"), 1)
                tt_.wait_ge(S("cb"), B("cb") + (1))
                tt_.matmul(pA[:, 0:512], trix[:], eqf[:, 0:512], start=True, stop=False)
                tt_.matmul(pA[:, 0:512], onesr[0:1, :], rowbuf2[0:1, 0:512], start=False,
                           stop=True)
                tt_.matmul(pC[:, 0:512], trix[:], eqf[:, 512:1024], start=True, stop=False)
                tt_.matmul(pC[:, 0:512], onesr[0:1, :], rowbuf2[0:1, 512:1024],
                           start=False, stop=True).then_inc(S("rk"), 1)

            @b.sync
            def _(s):
                s.wait_ge(S("qd"), B("qd") + (1))
                for g in range(8):
                    s.dma_start(out=qdram.rearrange("(r k c gg) -> r k c gg", r=16, k=2, c=32)[:, :, :, g],
                                in_=q16[g * 16:(g + 1) * 16, :, :],
                                allow_slow_non_contiguous=True).then_inc(S("tw"), 16)
                for g in range(8):
                    s.dma_start(out=gbdram.rearrange("(k r c gg) -> k r c gg", k=2, r=16, c=32)[0, :, :, g],
                                in_=q16[g * 16:(g + 1) * 16, 0, :],
                                allow_slow_non_contiguous=True).then_inc(S("tw"), 16)
                for g in range(8):
                    s.dma_start(out=gbdram.rearrange("(k r c gg) -> k r c gg", k=2, r=16, c=32)[1, :, :, g],
                                in_=q16[g * 16:(g + 1) * 16, 1, :],
                                allow_slow_non_contiguous=True).then_inc(S("tw"), 16)
                s.wait_ge(S("tw"), B("tw") + (16 * 24))
                s.dma_start(out=scat_idx[0:16, :],
                            in_=qdram.rearrange("(r ss) -> r ss", r=16)).then_inc(S("tr0"), 16)
                s.dma_start(out=gb_idx[0:16, 0:256],
                            in_=gbdram.rearrange("(k r ss) -> k r ss", k=2, r=16)[0]).then_inc(S("tr0"), 16)
                s.dma_start(out=gb_idx[0:16, 256:512],
                            in_=gbdram.rearrange("(k r ss) -> k r ss", k=2, r=16)[1]).then_inc(S("tr0"), 16)
                s.wait_ge(S("tr0"), B("tr0") + (48))
                for g in range(1, 8):
                    s.dma_start(out=scat_idx[16 * g:16 * (g + 1), :],
                                in_=scat_idx[0:16, :]).then_inc(S("tr"), 16)
                    s.dma_start(out=gb_idx[16 * g:16 * (g + 1), :],
                                in_=gb_idx[0:16, :]).then_inc(S("tr"), 16)

        if PHASES < 5:
            _pv = _trunc_prev(4)
            adv(FFN_TOT)
            return _pv
        with nc.Block(t + "xp") as b:
            @b.sync
            def _(s):
                for c in range(4):
                    if c >= 2:
                        s.wait_ge(S(f"sc{c % 2}"), B(f"sc{c % 2}") + (32 * (c // 2)))
                    s.dma_start(out=xeT[:, c % 2, :, :],
                                in_=xln[1024 * c:1024 * (c + 1), :].rearrange(
                                    "(cc p) d -> p cc d", p=128)).then_inc(S(f"dl{c % 2}"), 16)

            @b.gpsimd
            def _(g):
                g.wait_ge(S("tr"), B("tr") + (224))
                g.wait_ge(S("zero"), B("zero") + (16 * 40))
                for c in range(4):
                    g.wait_ge(S(f"dl{c % 2}"), B(f"dl{c % 2}") + (16 * (c // 2 + 1)))
                    for k in range(2):
                        g.dma_scatter_add(xe_s, xeT[:, c % 2, :, :],
                                          scat_idx[:, k * 256 + 64 * c:k * 256 + 64 * c + 64],
                                          1024, 1024, D,
                                          single_packet=False).then_inc(S(f"sc{c % 2}"), 16)
                g.wait_ge(S("sc0"), B("sc0") + (64))
                g.wait_ge(S("sc1"), B("sc1") + (64))

                def gath(gch):
                    if gch >= 2:
                        g.wait_ge(S("xeu"), B("xeu") + (8 * (gch - 1)))
                    g.dma_gather(xeT[:, gch % 2, :, :], xe_s,
                                 widx[:, 64 * gch:64 * (gch + 1)], 1024, 1024, D,
                                 transpose=True, single_packet=False).then_inc(S(f"gt{gch % 2}"), 16)

                def fetchD(jj):
                    er = g.value_load(tiletab[0:1, 160 + jj:161 + jj], min_val=0,
                                      max_val=E - 1)
                    fr = g.value_load(tiletab[0:1, 240 + jj:241 + jj], min_val=0,
                                      max_val=1)
                    g.dma_start(out=wbuf[:, jj % 2, 2, :, :],
                                in_=wgbf[2, er].rearrange("(dc p) h -> p dc h", p=128),
                                cond=fr).then_inc(S(f"wd{jj % 2}"), 16)

                gath(0)
                gath(1)
                g.wait_ge(S("tt"), B("tt") + (1))
                for e in range(E):
                    g.wait_ge(S(f"wce{e}"), 48)
                fetchD(0)
                fetchD(1)
                for j in range(NTILE):
                    if j + 2 < NTILE:
                        g.wait_ge(S("tl"), B("tl") + (j + 1))
                        fetchD(j + 2)
                    if j >= 6 and (j - 6) % 8 == 0 and (j - 6) // 8 + 2 <= 9:
                        gath((j - 6) // 8 + 2)
                    g.wait_ge(S("ve"), B("ve") + (3 * j + 3))
                    g.dma_start(out=outs_s[128 * j:128 * (j + 1), :],
                                in_=ob[:, j % 2, :]).then_inc(S(f"obw{j % 2}"), 16)

            @b.tensor
            def _(tt_):
                tt_.wait_ge(S("tt"), B("tt") + (1))

                def transp(k):
                    tt_.wait_ge(S("ve"), B("ve") + (3 * k + 1))
                    for hc in range(8):
                        ins = tt_.transpose(pbf[:, 128 * hc:128 * (hc + 1)],
                                            hid_sb[:, 128 * hc:128 * (hc + 1)], ident[:])
                    ins.then_inc(S("tr2"), 1)

                def down(k):
                    tt_.wait_ge(S("ve"), B("ve") + (3 * k + 2))
                    tt_.wait_ge(S(f"wd{k % 2}"), B(f"wd{k % 2}") + (16 * (k // 2 + 1)))
                    for hc in range(8):
                        st = hidT_sb[:, hc, :]
                        wsl = wbuf[:, k % 2, 2, hc, :]
                        tt_.matmul(pC[:, 0:512], st, wsl[:, 0:512], start=(hc == 0), stop=(hc == 7))
                        ins = tt_.matmul(pC[:, 512:1024], st, wsl[:, 512:1024],
                                         start=(hc == 0), stop=(hc == 7))
                    ins.then_inc(S("tl"), 1)

                for j in range(NTILE):
                    gch = j // 8
                    tt_.wait_ge(S(f"gt{gch % 2}"), B(f"gt{gch % 2}") + (16 * (gch // 2 + 1)))
                    tt_.wait_ge(S(f"wgu{j % 2}"), B(f"wgu{j % 2}") + (32 * (j // 2 + 1)))
                    if j >= 1:
                        tt_.wait_ge(S("si"), B("si") + (j))
                    li = j % 8
                    xs = xeT[:, (j // 8) % 2, :, :]
                    for dc in range(8):
                        st = xs[:, dc, 128 * li:128 * (li + 1)]
                        wsl = wbuf[:, j % 2, 0, dc, :]
                        tt_.matmul(pA[:, 0:512], st, wsl[:, 0:512], start=(dc == 0), stop=(dc == 7))
                        ins = tt_.matmul(pA[:, 512:1024], st, wsl[:, 512:1024],
                                         start=(dc == 0), stop=(dc == 7))
                    ins.then_inc(S("gd"), 1)
                    if j >= 1:
                        transp(j - 1)
                    for dc in range(8):
                        st = xs[:, dc, 128 * li:128 * (li + 1)]
                        wsl = wbuf[:, j % 2, 1, dc, :]
                        tt_.matmul(pB[:, 0:512], st, wsl[:, 0:512], start=(dc == 0), stop=(dc == 7))
                        ins = tt_.matmul(pB[:, 512:1024], st, wsl[:, 512:1024],
                                         start=(dc == 0), stop=(dc == 7))
                    ins.then_inc(S("xeu"), 1)
                    if j >= 1:
                        down(j - 1)
                transp(NTILE - 1)
                down(NTILE - 1)

            @b.scalar
            def _(sc):
                def fetchGU(jj):
                    er = sc.value_load(tiletab[0:1, 160 + jj:161 + jj], min_val=0,
                                       max_val=E - 1)
                    fr = sc.value_load(tiletab[0:1, 240 + jj:241 + jj], min_val=0,
                                       max_val=1)
                    for m in range(2):
                        sc.dma_start(out=wbuf[:, jj % 2, m, :, :],
                                     in_=wgbf[m, er].rearrange("(dc p) h -> p dc h", p=128),
                                     cond=fr).then_inc(S(f"wgu{jj % 2}"), 16)

                sc.wait_ge(S("tt"), B("tt") + (1))
                for e in range(E):
                    sc.wait_ge(S(f"wce{e}"), 48)
                fetchGU(0)
                fetchGU(1)
                for j in range(NTILE):
                    sc.wait_ge(S("gd"), B("gd") + (j + 1))
                    sc.activation(silu_sb[:], pA[:], AF.Silu, bias=0.0,
                                  scale=1.0).then_inc(S("si"), 1)
                    if j + 2 < NTILE:
                        sc.wait_ge(S("xeu"), B("xeu") + (j + 1))
                        fetchGU(j + 2)

            @b.vector
            def _(v):
                for j in range(NTILE):
                    v.wait_ge(S("si"), B("si") + (j + 1))
                    v.wait_ge(S("xeu"), B("xeu") + (j + 1))
                    v.tensor_mul(hid_sb[:], silu_sb[:], pB[:]).then_inc(S("ve"), 1)
                    v.wait_ge(S("tr2"), B("tr2") + (j + 1))
                    v.tensor_copy(hidT_sb[:].rearrange("p a bq -> p (a bq)"),
                                  pbf[:]).then_inc(S("ve"), 1)
                    v.wait_ge(S("tl"), B("tl") + (j + 1))
                    if j >= 2:
                        v.wait_ge(S(f"obw{j % 2}"), B(f"obw{j % 2}") + (16 * (j // 2)))
                    v.tensor_copy(ob[:, j % 2, :], pC[:]).then_inc(S("ve"), 1)

        if PHASES < 6:
            _pv = _trunc_prev(5)
            adv(FFN_TOT)
            return _pv
        with nc.Block(t + "cm") as b:
            @b.gpsimd
            def _(g):
                g.wait_ge(S("obw0"), B("obw0") + (16 * NTILE // 2))
                g.wait_ge(S("obw1"), B("obw1") + (16 * NTILE // 2))
                for c in range(4):
                    if c >= 1:
                        g.wait_ge(S("cmb"), B("cmb") + (8 * c))
                    g.dma_gather(xeT[:, 0, :, :], outs_s, gb_idx[:, 64 * c:64 * c + 64],
                                 1024, 1024, D, single_packet=False).then_inc(S("gb0"), 16)
                    g.dma_gather(xeT[:, 1, :, :], outs_s,
                                 gb_idx[:, 256 + 64 * c:256 + 64 * c + 64],
                                 1024, 1024, D, single_packet=False).then_inc(S("gb1"), 16)

            @b.sync
            def _(s):
                for i in range(NT):
                    if i >= 2:
                        s.wait_ge(S("cmb"), B("cmb") + (i - 1))
                    s.dma_start(out=xt2[:, i % 2, :], in_=src[i * 128:(i + 1) * 128, :]).then_inc(S(f"cx{i % 2}"), 16)

            @b.vector
            def _(v):
                for i in range(NT):
                    c = i // 8
                    v.wait_ge(S("gb0"), B("gb0") + (16 * (c + 1)))
                    v.wait_ge(S("gb1"), B("gb1") + (16 * (c + 1)))
                    v.wait_ge(S(f"cx{i % 2}"), B(f"cx{i % 2}") + (16 * (i // 2 + 1)))
                    if i >= 2:
                        v.wait_ge(S(f"ow{i % 2}"), B(f"ow{i % 2}") + (16 * (i // 2)))
                    li = i % 8
                    v.scalar_tensor_tensor(silu_sb[:],
                                           xeT[:, 0, li, :], w1a[:, i:i + 1],
                                           xt2[:, i % 2, :], AL.mult, AL.add)
                    vsp(v)
                    v.scalar_tensor_tensor(outt[:, i % 2, :], xeT[:, 1, li, :],
                                           w2a[:, i:i + 1], silu_sb[:], AL.mult,
                                           AL.add).then_inc(S("cmb"), 1)

            @b.scalar
            def _(sc):
                for i in range(NT):
                    sc.wait_ge(S("cmb"), B("cmb") + (i + 1))
                    sc.dma_start(out=dst[i * 128:(i + 1) * 128, :],
                                 in_=outt[:, i % 2, :]).then_inc(S(f"ow{i % 2}"), 16)

        adv(FFN_TOT)
        return [(S("ow0"), B("ow0")), (S("ow1"), B("ow1"))]

    def emit_attn(dst, prev):
        t = "a_"

        with nc.Block("a0") as b:
            @b.gpsimd
            def _(g):
                for pr in prev:
                    g.wait_ge(*pr)
                g.dma_start(out=at_wv2[:], in_=wv2_d.rearrange("(dc p) e -> p dc e", p=128)).then_inc(S("wl"), 16)
                g.dma_start(out=at_wo2[:], in_=wo2_d.rearrange("(dc p) e -> p dc e", p=128)).then_inc(S("wl"), 16)

            @b.sync
            def _(s):
                for pr in prev:
                    s.wait_ge(*pr)
                s.dma_start(out=at_abcsb[0:1, :],
                            in_=abc_d.rearrange("a d -> (a d)").rearrange("(o xx) -> o xx", o=1)).then_inc(S("wlh"), 16)

            @b.tensor
            def _(tt_):
                tt_.wait_ge(S("wl"), B("wl") + (32))
                tt_.wait_ge(S("wlh"), B("wlh") + (16))
                for r in range(5):
                    if r >= 1:
                        tt_.wait_ge(S("bce"), B("bce") + (r))
                    tt_.matmul(pA[:, 0:512], onesr[0:1, :],
                               at_abcsb[0:1, 1024 * r:1024 * r + 512], start=True, stop=True)
                    tt_.matmul(pA[:, 512:1024], onesr[0:1, :],
                               at_abcsb[0:1, 1024 * r + 512:1024 * (r + 1)],
                               start=True, stop=True).then_inc(S("bcm"), 1)

            @b.vector
            def _(v):
                for r in range(5):
                    v.wait_ge(S("bcm"), B("bcm") + (r + 1))
                    v.tensor_copy(at_abc[:, r, :], pA[:]).then_inc(S("bce"), 1)
                for pt in range(NSEQT):
                    v.scalar_tensor_tensor(at_sarg[:], at_abc[:, 0, :], ncol[:, pt:pt + 1],
                                           at_abc[:, 1, :], AL.mult, AL.add).then_inc(S("varg"), 1)
                    v.wait_ge(S("asin"), B("asin") + (2 * pt + 1))
                    v.tensor_mul(at_sarg[:], at_sinv[:], at_abc[:, 4, :])
                    vsp(v)
                    v.tensor_reduce(p1a[:, pt, :],
                                    at_sarg[:].rearrange("p (h q) -> p h q", h=H), AX, AL.add)
                    v.scalar_tensor_tensor(at_sarg[:], at_abc[:, 2, :], ncol[:, pt:pt + 1],
                                           at_abc[:, 3, :], AL.mult, AL.add).then_inc(S("varg"), 1)
                    v.wait_ge(S("asin"), B("asin") + (2 * pt + 2))
                    v.tensor_mul(at_sarg[:], at_sinv[:], at_abc[:, 4, :])
                    vsp(v)
                    v.tensor_reduce(at_p2r[:, pt, :],
                                    at_sarg[:].rearrange("p (h q) -> p h q", h=H), AX, AL.add)
                    vsp(v)
                    v.tensor_scalar(at_p2r[:, pt, :], at_p2r[:, pt, :], 20.0, -20.0,
                                    AL.min, AL.max).then_inc(S("vp2"), 1)
                v.memset(carry[0:1, :], 0.0).then_inc(S("car"), 1)

            @b.scalar
            def _(sc):
                for pt in range(NSEQT):
                    sc.wait_ge(S("varg"), B("varg") + (2 * pt + 1))
                    sc.activation(at_sinv[:], at_sarg[:], AF.Sin, bias=0.0,
                                  scale=1.0).then_inc(S("asin"), 1)
                    sc.wait_ge(S("varg"), B("varg") + (2 * pt + 2))
                    sc.activation(at_sinv[:], at_sarg[:], AF.Sin, bias=0.0,
                                  scale=1.0).then_inc(S("asin"), 1)
                sc.wait_ge(S("vp2"), B("vp2") + (NSEQT))
                sc.activation(p2ea[:].rearrange("p a bq -> p (a bq)"),
                              at_p2r[:].rearrange("p a bq -> p (a bq)"), AF.Exp,
                              bias=0.0, scale=1.0).then_inc(S("p2e"), 1)

        emit_ln(t, xres, prev, at_xt, at_lnout, at_sq)

        with nc.Block("a2") as b:
            @b.gpsimd
            def _(g):
                for gc in range(4):
                    for par in range(2):
                        g.wait_ge(S(f"lnw{par}"), B(f"lnw{par}") + (16 * (4 * (gc + 1))))
                    if gc >= 2:
                        g.wait_ge(S("avv"), B("avv") + (8 * (gc - 1)))
                    g.dma_gather(at_xlnT[:, gc % 2, :, :], xln,
                                 widx[:, 64 * gc:64 * (gc + 1)],
                                 1024, 1024, D, transpose=True, single_packet=False).then_inc(S(f"axg{gc % 2}"), 16)

            @b.tensor
            def _(tt_):
                tt_.wait_ge(S("p2e"), B("p2e") + (1))
                tt_.wait_ge(S("car"), B("car") + (1))
                for ti in range(2 * NSEQT):
                    gc = ti // 8
                    li = ti % 8
                    tt_.wait_ge(S(f"axg{gc % 2}"), B(f"axg{gc % 2}") + (16 * (gc // 2 + 1)))
                    if ti >= 1:
                        tt_.wait_ge(S("av1"), B("av1") + (ti))
                    if ti == NSEQT:
                        tt_.wait_ge(S("car"), B("car") + (2))
                    xsl = at_xlnT[:, gc % 2, :, :]
                    for dc in range(8):
                        st = xsl[:, dc, 128 * li:128 * (li + 1)]
                        tt_.matmul(pA[:, 0:512], st, at_wv2[:, dc, 0:512],
                                   start=(dc == 0), stop=(dc == 7))
                        tt_.matmul(pA[:, 512:1024], st, at_wv2[:, dc, 512:1024],
                                   start=(dc == 0), stop=(dc == 7))
                        ins = tt_.matmul(pD[:, 0:48], st, k123sb[:, dc, :],
                                         start=(dc == 0), stop=(dc == 7))
                    ins.then_inc(S("avv"), 1)
                    tt_.wait_ge(S("au"), B("au") + (ti + 1))
                    if ti >= 1:
                        tt_.wait_ge(S("anum"), B("anum") + (ti))
                        tt_.wait_ge(S("aca"), B("aca") + (ti))
                    cur = ti % 2
                    cb = 1088 * cur
                    tt_.matmul(pB[:, 0:512], tri[:], at_u[:, 0:512], start=True, stop=False)
                    tt_.matmul(pB[:, 0:512], onesr[0:1, :], carry[0:1, cb:cb + 512],
                               start=False, stop=True)
                    tt_.matmul(pB[:, 512:1024], tri[:], at_u[:, 512:1024], start=True, stop=False)
                    tt_.matmul(pB[:, 512:1024], onesr[0:1, :], carry[0:1, cb + 512:cb + 1024],
                               start=False, stop=True)
                    tt_.matmul(pD[:, 64:80], tri[:], at_ud[:], start=True, stop=False)
                    tt_.matmul(pD[:, 64:80], onesr[0:1, :], carry[0:1, cb + 1024:cb + 1040],
                               start=False, stop=True).then_inc(S("asm"), 1)
                    if ti >= 1:
                        tt_.wait_ge(S("acse"), B("acse") + (ti))
                    tt_.matmul(pC[0:1, 0:512], onesc[:], at_u[:, 0:512], start=True, stop=True)
                    tt_.matmul(pC[0:1, 512:1024], onesc[:], at_u[:, 512:1024], start=True, stop=True)
                    tt_.matmul(pD[0:1, 96:112], onesc[:], at_ud[:], start=True,
                               stop=True).then_inc(S("acs"), 1)
                    tt_.wait_ge(S("anum"), B("anum") + (ti + 1))
                    for hc in range(8):
                        ins = tt_.transpose(pbf[:, 128 * hc:128 * (hc + 1)],
                                            at_outv[:, 128 * hc:128 * (hc + 1)], ident[:])
                    ins.then_inc(S("atr"), 1)
                    tt_.wait_ge(S("aovT"), B("aovT") + (ti + 1))
                    for hc in range(8):
                        st = at_outvT[:, hc, :]
                        tt_.matmul(pA[:, 0:512], st, at_wo2[:, hc, 0:512],
                                   start=(hc == 0), stop=(hc == 7))
                        ins = tt_.matmul(pA[:, 512:1024], st, at_wo2[:, hc, 512:1024],
                                         start=(hc == 0), stop=(hc == 7))
                    ins.then_inc(S("ao"), 1)

            @b.vector
            def _(v):
                for ti in range(2 * NSEQT):
                    s, i = divmod(ti, NSEQT)
                    if ti == NSEQT:
                        v.wait_ge(S("asm"), B("asm") + (NSEQT))
                        v.memset(carry[0:1, :], 0.0).then_inc(S("car"), 1)
                    v.wait_ge(S("avv"), B("avv") + (ti + 1))
                    v.tensor_add(at_cde[:, 0:16], pD[:, 0:16], p1a[:, i, :])
                    v.tensor_copy(at_cde[:, 16:48], pD[:, 16:48])
                    vsp(v)
                    v.tensor_scalar(at_cde2[:], at_cde[:], 20.0, -20.0, AL.min,
                                    AL.max).then_inc(S("acd"), 1)
                    v.wait_ge(S("ace"), B("ace") + (ti + 1))
                    v.tensor_tensor(at_u[:].rearrange("p (h q) -> p h q", h=H),
                                    pA[:].rearrange("p (h q) -> p h q", h=H),
                                    at_eE[:, 0:16].unsqueeze(2).broadcast_to([128, 16, 64]),
                                    AL.mult)
                    v.tensor_tensor(at_tvd[:].rearrange("p (h q) -> p h q", h=H),
                                    pA[:].rearrange("p (h q) -> p h q", h=H),
                                    at_eE[:, 16:32].unsqueeze(2).broadcast_to([128, 16, 64]),
                                    AL.mult)
                    v.tensor_copy(at_ud[:], at_eE[:, 0:16])
                    v.tensor_mul(at_peee[:], p2ea[:, i, :], at_eE[:, 32:48]).then_inc(S("au"), 1)
                    v.wait_ge(S("acs"), B("acs") + (ti + 1))
                    v.tensor_copy(rowbuf2[0:1, 0:1024], pC[0:1, :])
                    v.tensor_copy(rowbuf2[0:1, 1024:1040], pD[0:1, 96:112]).then_inc(S("acse"), 1)
                    vsp(v)
                    cur, nxt = ti % 2, (ti + 1) % 2
                    v.tensor_add(carry[0:1, 1088 * nxt:1088 * nxt + 1040],
                                 carry[0:1, 1088 * cur:1088 * cur + 1040],
                                 rowbuf2[0:1, 0:1040]).then_inc(S("aca"), 1)
                    v.wait_ge(S("asm"), B("asm") + (ti + 1))
                    v.tensor_tensor(at_num[:].rearrange("p (h q) -> p h q", h=H),
                                    pB[:].rearrange("p (h q) -> p h q", h=H),
                                    at_peee[:].unsqueeze(2).broadcast_to([128, 16, 64]),
                                    AL.mult)
                    vsp(v)
                    v.tensor_add(at_num2[:], at_num[:], at_tvd[:])
                    v.tensor_mul(at_den[:], pD[:, 64:80], at_peee[:])
                    vsp(v)
                    v.tensor_add(at_den2[:], at_den[:], at_eE[:, 16:32])
                    vsp(v)
                    v.reciprocal(at_rden[:], at_den2[:])
                    vsp(v)
                    v.tensor_tensor(at_outv[:].rearrange("p (h q) -> p h q", h=H),
                                    at_num2[:].rearrange("p (h q) -> p h q", h=H),
                                    at_rden[:].unsqueeze(2).broadcast_to([128, 16, 64]),
                                    AL.mult).then_inc(S("anum"), 1)
                    v.wait_ge(S("atr"), B("atr") + (ti + 1))
                    v.tensor_copy(at_outvT[:].rearrange("p a bq -> p (a bq)"),
                                  pbf[:]).then_inc(S("aovT"), 1)
                    v.wait_ge(S("ao"), B("ao") + (ti + 1))
                    v.wait_ge(S(f"axl{ti % 2}"), B(f"axl{ti % 2}") + (16 * (ti // 2 + 1)))
                    if ti >= 2:
                        v.wait_ge(S(f"aw{ti % 2}"), B(f"aw{ti % 2}") + (16 * (ti // 2)))
                    v.tensor_add(at_o[:, ti % 2, :], pA[:], at_xt[:, ti % 2, :]).then_inc(S("av1"), 1)

            @b.scalar
            def _(sc):
                for ti in range(2 * NSEQT):
                    sc.wait_ge(S("acd"), B("acd") + (ti + 1))
                    sc.activation(at_eE[:], at_cde2[:], AF.Exp, bias=0.0,
                                  scale=1.0).then_inc(S("ace"), 1)
                    sc.wait_ge(S("av1"), B("av1") + (ti + 1))
                    sc.dma_start(out=dst[128 * ti:128 * (ti + 1), :],
                                 in_=at_o[:, ti % 2, :]).then_inc(S(f"aw{ti % 2}"), 16)

            @b.sync
            def _(s):
                s.wait_ge(S("ln"), B("ln") + (NT))
                for ti in range(2 * NSEQT):
                    if ti >= 2:
                        s.wait_ge(S("av1"), B("av1") + (ti - 1))
                    s.dma_start(out=at_xt[:, ti % 2, :],
                                in_=xres[128 * ti:128 * (ti + 1), :]).then_inc(S(f"axl{ti % 2}"), 16)

        adv(ATTN_TOT)
        return [(S("aw0"), B("aw0")), (S("aw1"), B("aw1"))]

    prev = [(S("v_init"), 1)]
    if stage == 1:
        prev = emit_ffn("f0", x_in, out_d, prev, 0)
    elif stage == 2:
        prev = emit_ffn("f0", x_in, xres, prev, 0)
        prev = emit_attn(out_d, prev)
    else:
        prev = emit_ffn("f0", x_in, xres, prev, 0)
        prev = emit_attn(xres, prev)
        prev = emit_ffn("f1", xres, out_d, prev, 1)

    with nc.Block("fin") as b:
        @b.sync
        def _(s):
            for pr in prev:
                s.wait_ge(*pr)

        @b.gpsimd
        def _(g):
            for pr in prev:
                g.wait_ge(*pr)

    return nc


def kernel(**inputs):
    x = np.ascontiguousarray(np.asarray(inputs["L0_attn_x"], dtype=np.float32))
    k1 = np.asarray(inputs["L0_attn_k1"], np.float32)
    k2 = np.asarray(inputs["L0_attn_k2"], np.float32)
    k3 = np.asarray(inputs["L0_attn_k3"], np.float32)
    a1 = np.asarray(inputs["L0_attn_a1"], np.float32).reshape(-1)
    a2 = np.asarray(inputs["L0_attn_a2"], np.float32).reshape(-1)
    b1 = np.asarray(inputs["L0_attn_b1"], np.float32).reshape(-1)
    b2 = np.asarray(inputs["L0_attn_b2"], np.float32).reshape(-1)
    cc = np.asarray(inputs["L0_attn_c"], np.float32).reshape(-1)
    vw = np.asarray(inputs["L0_attn_value_weight"], np.float32)
    owt = np.asarray(inputs["L0_attn_output_weight"], np.float32)
    router = np.ascontiguousarray(np.asarray(inputs["L0_ffn_router_w"], np.float32))
    gate = np.ascontiguousarray(np.asarray(inputs["L0_ffn_gate_w"], np.float32))
    up = np.ascontiguousarray(np.asarray(inputs["L0_ffn_up_w"], np.float32))
    down = np.ascontiguousarray(np.asarray(inputs["L0_ffn_down_w"], np.float32))

    wv2 = np.ascontiguousarray(vw.transpose(1, 0, 2).reshape(D, D))
    wo2 = np.ascontiguousarray(owt.transpose(0, 2, 1).reshape(D, D))
    k123 = np.ascontiguousarray(np.concatenate([k1.T, k2.T, k3.T], axis=1))
    abc = np.ascontiguousarray(np.stack([a1, b1, a2, b2, cc], axis=0))

    nc = build(STAGE)
    nc.finalize()

    in_maps = []
    for c in range(NCORES):
        xs = np.ascontiguousarray(x[2 * c:2 * c + 2].reshape(T, D))
        in_maps.append({
            "x": xs, "router_w": router, "gate_w": gate, "up_w": up, "down_w": down,
            "wv2": wv2, "wo2": wo2, "k123": k123, "abc": abc,
        })
    res = run_bass_kernel_spmd(nc, in_maps, core_ids=list(range(NCORES)))
    out = np.stack([r["out"].reshape(2, SEQ, D) for r in res.results], axis=0)
    return out.reshape(NCORES * 2, SEQ, D).astype(np.float32)
